# version 24
# baseline (speedup 1.0000x reference)
import numpy as np
import concourse.bass as bass
import concourse.mybir as mybir
from concourse.bass_utils import run_bass_kernel_spmd
from concourse.aot_env import allow_ici_all_to_all

F32 = mybir.dt.float32
BF16 = mybir.dt.bfloat16
U16 = mybir.dt.uint16
AF = mybir.ActivationFunctionType
OP = mybir.AluOpType

B, L, DM, DI, DS, DC, FI = 4, 8192, 1024, 2048, 16, 4, 2816
NCORES = 8
RPC = B * L // NCORES
DPC = DI // NCORES
RH = RPC + 16
EPS = 1e-5

FC = [(i * 512, 512) for i in range(8)] + [(4096, 16)]
FC2 = [(i * 512, 512) for i in range(8)]
MAGIC = 16777216.0
GUARD = 1.0 - 2e-7


class Sem:
    def __init__(self, h):
        self.h = h
        self.n = 0


class Prog:
    ENGS = ("sync", "vector", "scalar", "tensor", "gpsimd")

    def __init__(self):
        self.items = []

    def op(self, eng, fn, waits=(), incs=()):
        assert eng in self.ENGS
        self.items.append((eng, fn, tuple(waits), tuple(incs)))
        for s, amt in incs:
            s.n += (16 if amt == 16 else (1 if amt == 1 else 0))

    def emit(self, block):
        def section(eng_name):
            def _(eng):
                for e, fn, waits, incs in self.items:
                    if e != eng_name:
                        continue
                    for s, v in waits:
                        if v > 0:
                            eng.wait_ge(s.h, v)
                    ins = fn(eng)
                    for s, amt in incs:
                        if amt is None:
                            ins.then_inc(s.h)
                        else:
                            ins.then_inc(s.h, amt)
            return _
        block.sync(section("sync"))
        block.vector(section("vector"))
        block.scalar(section("scalar"))
        block.tensor(section("tensor"))
        block.gpsimd(section("gpsimd"))


def build_program(dbg_names=()):
    nc = bass.Bass("TRN2", target_bir_lowering=False, debug=False)
    p = Prog()

    xin = nc.dram_tensor("x_rows", [RH, DM], F32, kind="ExternalInput").ap()
    w_in = nc.dram_tensor("w_in", [2 * DI, DM], F32, kind="ExternalInput").ap()
    conv_w = nc.dram_tensor("conv_w", [128, 16 * DC], F32, kind="ExternalInput").ap()
    conv_b = nc.dram_tensor("conv_b", [128, 16], F32, kind="ExternalInput").ap()
    w_xp = nc.dram_tensor("w_xp", [2 * DS + 1, DI], F32, kind="ExternalInput").ap()
    dt_w = nc.dram_tensor("dt_w", [128, 2], F32, kind="ExternalInput").ap()
    dt_b = nc.dram_tensor("dt_b", [128, 2], F32, kind="ExternalInput").ap()
    a_log = nc.dram_tensor("a_log", [128, 2 * DS], F32, kind="ExternalInput").ap()
    d_par = nc.dram_tensor("d_par", [128, 2], F32, kind="ExternalInput").ap()
    w_out = nc.dram_tensor("w_out", [DM, DI], F32, kind="ExternalInput").ap()
    w_g = nc.dram_tensor("w_g", [FI, DM], F32, kind="ExternalInput").ap()
    w_u = nc.dram_tensor("w_u", [FI, DM], F32, kind="ExternalInput").ap()
    w_d = nc.dram_tensor("w_d", [DM, FI], F32, kind="ExternalInput").ap()
    out = nc.dram_tensor("out", [RPC, DM], F32, kind="ExternalOutput").ap()

    d_xb = nc.dram_tensor("d_xb", [RH, DM], BF16).ap()
    d_winb = nc.dram_tensor("d_winb", [2 * DI, DM], BF16).ap()
    d_wxpb = nc.dram_tensor("d_wxpb", [48, DI], BF16).ap()
    d_woutb = nc.dram_tensor("d_woutb", [DM, DI], BF16).ap()
    d_wqgb = nc.dram_tensor("d_wqgb", [FI, DM], BF16).ap()
    d_wqub = nc.dram_tensor("d_wqub", [FI, DM], BF16).ap()
    d_wqdb = nc.dram_tensor("d_wqdb", [DM, FI], BF16).ap()
    d_ain = nc.dram_tensor("d_ain", [DI, RPC], BF16).ap()
    d_aout = nc.dram_tensor("d_aout", [DI, RPC], BF16).ap()
    d_z = nc.dram_tensor("d_z", [DI, RPC], BF16).ap()
    d_bcdo = nc.dram_tensor("d_bcdo", [2 * DS + 1, RPC], F32).ap()
    d_bcda = nc.dram_tensor("d_bcda", [NCORES * (2 * DS + 1), RPC], F32).ap()
    d_bcdb = nc.dram_tensor("d_bcdb", [NCORES * 2 * DS, RPC], BF16).ap()
    d_yin = nc.dram_tensor("d_yin", [DI, RPC], BF16).ap()
    d_yout = nc.dram_tensor("d_yout", [DI, RPC], BF16).ap()
    d_x1 = nc.dram_tensor("d_x1", [RPC, DM], BF16).ap()
    d_xq = nc.dram_tensor("d_xq", [RPC, DM], BF16).ap()
    d_hq = nc.dram_tensor("d_hq", [RPC, FI], BF16).ap()

    dbg = {}
    def dbg_tap(name, shape, dtype=F32):
        if name in dbg_names:
            dbg[name] = nc.dram_tensor("dbg_" + name, shape, dtype,
                                       kind="ExternalOutput").ap()
        return dbg.get(name)

    import contextlib
    stack = contextlib.ExitStack()
    nc._keepalive = stack
    phase_stack = [contextlib.ExitStack()]

    def sem(name):
        return Sem(stack.enter_context(nc.semaphore(name)))

    def sbuf(name, shape, dtype):
        return phase_stack[0].enter_context(nc.sbuf_tensor(name, shape, dtype))

    def psum(name, shape, dtype=F32):
        return phase_stack[0].enter_context(nc.psum_tensor(name, shape, dtype))

    def phase_end():
        phase_stack[0].close()
        phase_stack[0] = contextlib.ExitStack()

    s_dma = sem("s_dma")
    s_dmas = sem("s_dmas")
    s_pe = sem("s_pe")
    s_dve = sem("s_dve")
    s_act = sem("s_act")
    s_dmb = sem("s_dmb")
    s_cc = [sem(f"s_cc{i}") for i in range(3)]
    s_out = sem("s_out")
    s_hq = sem("s_hq")
    s_cx = sem("s_cx")
    s_wo = sem("s_wo")
    dly = stack.enter_context(nc.sbuf_tensor("dly", [128, 256], F32))

    def emit_quant(pp, Fq, x_f32, qout_bf, t, waits_in):
        nb = Fq // 16
        wq0 = list(waits_in) + [(s_dve, s_dve.n)]
        p.op("scalar", lambda e: e.activation(out=t["qg"][:pp, :Fq],
                                              in_=x_f32, func=AF.Sign),
             waits=wq0, incs=[(s_act, 1)])
        p.op("scalar", lambda e: e.activation(out=t["qa"][:pp, :Fq],
                                              in_=x_f32, func=AF.Abs),
             waits=wq0, incs=[(s_act, 1)])
        p.op("vector", lambda e: e.reduce_max(
            out=t["qm"][:pp, :nb].unsqueeze(2),
            in_=t["qa"][:pp, :Fq].rearrange("p (r n) -> p r n", n=16),
            axis=mybir.AxisListType.X),
            waits=waits_in + [(s_act, s_act.n)], incs=[(s_dve, 1)])
        p.op("vector", lambda e: e.tensor_scalar(
            out=t["qs"][:pp, :nb], in0=t["qm"][:pp, :nb],
            scalar1=1.0 / 6.0, scalar2=1e-12, op0=OP.mult, op1=OP.max),
            waits=[(s_dve, s_dve.n)], incs=[(s_dve, 1)])
        p.op("vector", lambda e: e.reciprocal(
            out=t["qi"][:pp, :nb], in_=t["qs"][:pp, :nb]),
            waits=[(s_dve, s_dve.n)], incs=[(s_dve, 1)])
        p.op("vector", lambda e: e.tensor_single_scalar(
            out=t["qsb"][:pp, :nb], in_=t["qs"][:pp, :nb], scalar=0.25,
            op=OP.mult), waits=[(s_dve, s_dve.n)], incs=[(s_dve, 1)])

        def bc(ap):
            return ap.unsqueeze(2).broadcast_to([pp, nb, 16])
        qa3 = t["qa"][:pp, :Fq].rearrange("p (r n) -> p r n", n=16)
        qb3 = t["qb"][:pp, :Fq].rearrange("p (r n) -> p r n", n=16)
        qg3 = t["qg"][:pp, :Fq].rearrange("p (r n) -> p r n", n=16)
        p.op("vector", lambda e: e.tensor_tensor(
            out=qb3, in0=qa3, in1=bc(t["qi"][:pp, :nb]), op=OP.mult),
            waits=[(s_dve, s_dve.n)], incs=[(s_dve, 1)])
        p.op("vector", lambda e: e.tensor_single_scalar(
            out=t["qc"][:pp, :Fq], in_=t["qb"][:pp, :Fq], scalar=2.0,
            op=OP.is_ge), waits=[(s_dve, s_dve.n)], incs=[(s_dve, 1)])
        p.op("vector", lambda e: e.tensor_single_scalar(
            out=t["qd"][:pp, :Fq], in_=t["qb"][:pp, :Fq], scalar=4.0,
            op=OP.is_ge), waits=[(s_dve, s_dve.n)], incs=[(s_dve, 1)])
        p.op("vector", lambda e: e.scalar_tensor_tensor(
            out=t["qe"][:pp, :Fq], in0=t["qc"][:pp, :Fq], scalar=-2.0,
            in1=t["qd"][:pp, :Fq], op0=OP.mult, op1=OP.subtract),
            waits=[(s_dve, s_dve.n)], incs=[(s_dve, 1)])
        p.op("vector", lambda e: e.tensor_single_scalar(
            out=t["qe"][:pp, :Fq], in_=t["qe"][:pp, :Fq], scalar=4.0,
            op=OP.add), waits=[(s_dve, s_dve.n)], incs=[(s_dve, 1)])
        p.op("vector", lambda e: e.tensor_tensor(
            out=t["qb"][:pp, :Fq], in0=t["qb"][:pp, :Fq],
            in1=t["qe"][:pp, :Fq], op=OP.mult),
            waits=[(s_dve, s_dve.n)], incs=[(s_dve, 1)])
        p.op("vector", lambda e: e.tensor_scalar(
            out=t["qb"][:pp, :Fq], in0=t["qb"][:pp, :Fq],
            scalar1=GUARD, scalar2=MAGIC, op0=OP.mult, op1=OP.add),
            waits=[(s_dve, s_dve.n)], incs=[(s_dve, 1)])
        p.op("vector", lambda e: e.tensor_single_scalar(
            out=t["qb"][:pp, :Fq], in_=t["qb"][:pp, :Fq], scalar=MAGIC,
            op=OP.subtract), waits=[(s_dve, s_dve.n)], incs=[(s_dve, 1)])
        p.op("vector", lambda e: e.scalar_tensor_tensor(
            out=t["qc"][:pp, :Fq], in0=t["qd"][:pp, :Fq], scalar=2.0,
            in1=t["qc"][:pp, :Fq], op0=OP.mult, op1=OP.add),
            waits=[(s_dve, s_dve.n)], incs=[(s_dve, 1)])
        p.op("vector", lambda e: e.tensor_single_scalar(
            out=t["qc"][:pp, :Fq], in_=t["qc"][:pp, :Fq], scalar=1.0,
            op=OP.add), waits=[(s_dve, s_dve.n)], incs=[(s_dve, 1)])
        p.op("vector", lambda e: e.tensor_tensor(
            out=t["qb"][:pp, :Fq], in0=t["qb"][:pp, :Fq],
            in1=t["qc"][:pp, :Fq], op=OP.mult),
            waits=[(s_dve, s_dve.n)], incs=[(s_dve, 1)])
        p.op("vector", lambda e: e.tensor_tensor(
            out=qa3, in0=qb3, in1=bc(t["qsb"][:pp, :nb]), op=OP.mult),
            waits=[(s_dve, s_dve.n)], incs=[(s_dve, 1)])
        p.op("vector", lambda e: e.tensor_tensor(
            out=qout_bf.rearrange("p (r n) -> p r n", n=16), in0=qa3,
            in1=qg3, op=OP.mult),
            waits=[(s_dve, s_dve.n), (s_act, s_act.n)], incs=[(s_dve, 1)])

    def emit_quant2_head(pp, Fq, x_f32, t, waits_in):
        p.op("scalar", lambda e: e.activation(out=t["qg"][:pp, :Fq],
                                              in_=x_f32, func=AF.Sign),
             waits=list(waits_in), incs=[(s_act, 1)])
        p.op("scalar", lambda e: e.activation(out=t["qa"][:pp, :Fq],
                                              in_=x_f32, func=AF.Abs),
             waits=[], incs=[(s_act, 1)])
        return s_act.n

    def emit_quant2_rest(pp, Fq, qout_bf, t, abs_mark, waits_in):
        nb = Fq // 16
        p.op("vector", lambda e: e.reduce_max(
            out=t["qm"][:pp, :nb].unsqueeze(2),
            in_=t["qa"][:pp, :Fq].rearrange("p (r n) -> p r n", n=16),
            axis=mybir.AxisListType.X),
            waits=list(waits_in) + [(s_act, abs_mark)], incs=[(s_dve, 1)])
        p.op("vector", lambda e: e.tensor_scalar(
            out=t["qs"][:pp, :nb], in0=t["qm"][:pp, :nb],
            scalar1=1.0 / 6.0, scalar2=1e-12, op0=OP.mult, op1=OP.max),
            waits=[(s_dve, s_dve.n)], incs=[(s_dve, 1)])
        p.op("vector", lambda e: e.reciprocal(
            out=t["qi"][:pp, :nb], in_=t["qs"][:pp, :nb]),
            waits=[(s_dve, s_dve.n)], incs=[(s_dve, 1)])

        def bc(ap):
            return ap.unsqueeze(2).broadcast_to([pp, nb, 16])
        qa3 = t["qa"][:pp, :Fq].rearrange("p (r n) -> p r n", n=16)
        qb3 = t["qb"][:pp, :Fq].rearrange("p (r n) -> p r n", n=16)
        qg3 = t["qg"][:pp, :Fq].rearrange("p (r n) -> p r n", n=16)
        p.op("vector", lambda e: e.tensor_tensor(
            out=qb3, in0=qa3, in1=bc(t["qi"][:pp, :nb]), op=OP.mult),
            waits=[(s_dve, s_dve.n)], incs=[(s_dve, 1)])
        p.op("vector", lambda e: e.tensor_scalar(
            out=t["qe"][:pp, :Fq], in0=t["qb"][:pp, :Fq], scalar1=2.0,
            scalar2=-2.0, op0=OP.is_ge, op1=OP.mult),
            waits=[(s_dve, s_dve.n)], incs=[(s_dve, 1)])
        p.op("vector", lambda e: e.scalar_tensor_tensor(
            out=t["qe"][:pp, :Fq], in0=t["qb"][:pp, :Fq], scalar=4.0,
            in1=t["qe"][:pp, :Fq], op0=OP.is_ge, op1=OP.subtract),
            waits=[(s_dve, s_dve.n)], incs=[(s_dve, 1)])
        p.op("vector", lambda e: e.tensor_scalar(
            out=t["qe"][:pp, :Fq], in0=t["qe"][:pp, :Fq], scalar1=-1.0,
            scalar2=4.0, op0=OP.mult, op1=OP.add),
            waits=[(s_dve, s_dve.n)], incs=[(s_dve, 1)])
        p.op("vector", lambda e: e.tensor_tensor(
            out=t["qb"][:pp, :Fq], in0=t["qb"][:pp, :Fq],
            in1=t["qe"][:pp, :Fq], op=OP.mult),
            waits=[(s_dve, s_dve.n)], incs=[(s_dve, 1)])
        p.op("vector", lambda e: e.tensor_scalar(
            out=t["qb"][:pp, :Fq], in0=t["qb"][:pp, :Fq],
            scalar1=GUARD, scalar2=MAGIC, op0=OP.mult, op1=OP.add),
            waits=[(s_dve, s_dve.n)], incs=[(s_dve, 1)])
        p.op("vector", lambda e: e.tensor_single_scalar(
            out=t["qb"][:pp, :Fq], in_=t["qb"][:pp, :Fq], scalar=MAGIC,
            op=OP.subtract), waits=[(s_dve, s_dve.n)], incs=[(s_dve, 1)])

        def _recip_bf(e):
            with nc.allow_low_precision(reason="recip of {1,2,4} is exact"):
                return e.reciprocal(out=t["qe"][:pp, :Fq],
                                    in_=t["qe"][:pp, :Fq])
        p.op("vector", _recip_bf, waits=[(s_dve, s_dve.n)],
             incs=[(s_dve, 1)])
        p.op("vector", lambda e: e.tensor_tensor(
            out=t["qb"][:pp, :Fq], in0=t["qb"][:pp, :Fq],
            in1=t["qe"][:pp, :Fq], op=OP.mult),
            waits=[(s_dve, s_dve.n)], incs=[(s_dve, 1)])
        p.op("vector", lambda e: e.tensor_tensor(
            out=qa3, in0=qb3, in1=bc(t["qs"][:pp, :nb]), op=OP.mult),
            waits=[(s_dve, s_dve.n)], incs=[(s_dve, 1)])
        p.op("vector", lambda e: e.tensor_tensor(
            out=qout_bf.rearrange("p (r n) -> p r n", n=16), in0=qa3,
            in1=qg3, op=OP.mult),
            waits=[(s_dve, s_dve.n)], incs=[(s_dve, 1)])
        return s_dve.n

    def mk_qt2(tag, width):
        return {
            "qa": sbuf("qa" + tag, [128, width], F32).ap(),
            "qb": sbuf("qb" + tag, [128, width], F32).ap(),
            "qe": sbuf("qe" + tag, [128, width], BF16).ap(),
            "qg": sbuf("qg" + tag, [128, width], BF16).ap(),
            "qm": sbuf("qm" + tag, [128, width // 16], F32).ap(),
            "qs": sbuf("qs" + tag, [128, width // 16], F32).ap(),
            "qi": sbuf("qi" + tag, [128, width // 16], F32).ap(),
        }

    def mk_qt(tag, width):
        return {
            "qa": sbuf("qa" + tag, [128, width], F32).ap(),
            "qb": sbuf("qb" + tag, [128, width], F32).ap(),
            "qc": sbuf("qc" + tag, [128, width], BF16).ap(),
            "qd": sbuf("qd" + tag, [128, width], BF16).ap(),
            "qe": sbuf("qe" + tag, [128, width], BF16).ap(),
            "qm": sbuf("qm" + tag, [128, width // 16], F32).ap(),
            "qs": sbuf("qs" + tag, [128, width // 16], F32).ap(),
            "qi": sbuf("qi" + tag, [128, width // 16], F32).ap(),
            "qsb": sbuf("qsb" + tag, [128, width // 16], F32).ap(),
            "qg": sbuf("qg" + tag, [128, width], BF16).ap(),
        }

    p0a = sbuf("p0a", [128, 2 * 4096], F32)
    p0b = sbuf("p0b", [128, 2 * 4096], BF16)
    QT = mk_qt("0", 4096)

    cw_state = {"ld": [], "cv": [], "st": []}

    def conv_w32(src, dst, rows, cols, quant=False):
        if not quant:
            p.op("gpsimd", lambda e, src=src, dst=dst: e.dma_start(
                out=dst, in_=src), waits=[], incs=[(s_dmas, 16)])
            return
        ld, cv, st = cw_state["ld"], cw_state["cv"], cw_state["st"]
        ntile = (rows + 127) // 128
        for i in range(ntile):
            k = len(ld)
            par = k % 2
            r0 = i * 128
            pp = min(128, rows - r0)
            asl = p0a[:pp, par * 4096:par * 4096 + cols]
            bsl = p0b[:pp, par * 4096:par * 4096 + cols]
            w = []
            if k >= 2:
                w.append((s_dve, cv[k - 2]))
                w.append((s_dmas, st[k - 2]))
            p.op("sync", lambda e, r0=r0, pp=pp, src=src, cols=cols, asl=asl:
                 e.dma_start(out=asl, in_=src[r0:r0 + pp, :]),
                 waits=w, incs=[(s_dma, 16)])
            ld.append(s_dma.n)
            if quant:
                emit_quant(pp, cols, asl, bsl, QT, [(s_dma, ld[-1])])
            else:
                p.op("vector", lambda e, asl=asl, bsl=bsl: e.tensor_copy(
                    out=bsl, in_=asl),
                    waits=[(s_dma, ld[-1])], incs=[(s_dve, 1)])
            cv.append(s_dve.n)
            p.op("gpsimd", lambda e, r0=r0, pp=pp, dst=dst, bsl=bsl:
                 e.dma_start(out=dst[r0:r0 + pp, :], in_=bsl),
                 waits=[(s_dve, cv[-1])], incs=[(s_dmas, 16)])
            st.append(s_dmas.n)

    p.op("vector", lambda e: e.memset(dly[:, :], 0.0), waits=[],
         incs=[(s_dve, 1)])
    conv_w32(xin, d_xb, RH, DM)
    conv_w32(w_in, d_winb, 2 * DI, DM)
    conv_w32(w_xp, d_wxpb[0:2 * DS + 1, :], 2 * DS + 1, DI)
    conv_w32(w_out, d_woutb, DM, DI)
    conv_w32(w_g, d_wqgb, FI, DM, quant=True)
    conv_w32(w_u, d_wqub, FI, DM, quant=True)
    conv_w32(w_d, d_wqdb, DM, FI, quant=True)
    ph0 = [(s_dmas, s_dmas.n), (s_dma, s_dma.n), (s_dve, s_dve.n),
           (s_act, s_act.n)]
    phase_end()

    xT = sbuf("xT", [128, 8 * RH], BF16)
    winT = sbuf("winT", [128, 8 * 4096], BF16)
    wxpT = sbuf("wxpT", [128, 16 * 48], BF16)
    xpre = sbuf("xpre", [128, RH], BF16)
    cvA = sbuf("cvA", [128, 4096], F32)
    cvB = sbuf("cvB", [128, 4096], BF16)
    xt_t = sbuf("xt_t", [128, 2 * 4096], BF16)
    zbuf = sbuf("zbuf", [128, RH], BF16)
    bcda = sbuf("bcda", [33, 4096], F32)
    cwt = sbuf("cwt", [128, 16 * DC], F32)
    cbt = sbuf("cbt", [128, 16], F32)
    p_in = [psum("p_in0", [128, 512]), psum("p_in1", [128, 512])]
    p_bcd = [psum("p_bcd0", [33, 512]), psum("p_bcd1", [33, 512])]

    for kc in range(8):
        p.op("sync", lambda e, kc=kc: e.dma_start(
            out=xT[:, kc * RH:(kc + 1) * RH],
            in_=d_xb[:, kc * 128:(kc + 1) * 128], transpose=True),
            waits=ph0, incs=[(s_dma, 16)])
        p.op("sync", lambda e, kc=kc: e.dma_start(
            out=winT[:, kc * 4096:(kc + 1) * 4096],
            in_=d_winb[:, kc * 128:(kc + 1) * 128], transpose=True),
            waits=ph0, incs=[(s_dma, 16)])
    for kc in range(16):
        p.op("sync", lambda e, kc=kc: e.dma_start(
            out=wxpT[:, kc * 48:(kc + 1) * 48],
            in_=d_wxpb[:, kc * 128:(kc + 1) * 128], transpose=True),
            waits=ph0, incs=[(s_dma, 16)])
    p.op("sync", lambda e: e.dma_start(out=cwt[:, :], in_=conv_w),
         waits=ph0, incs=[(s_dmb, 16)])
    p.op("sync", lambda e: e.dma_start(out=cbt[:, :], in_=conv_b),
         waits=ph0, incs=[(s_dmb, 16)])
    cw_ld = s_dmb.n
    p.op("scalar", lambda e: e.activation(
        out=dly[:, :], in_=dly[:, :], func=AF.Square),
        waits=[(s_dma, s_dma.n)], incs=[(s_act, 1)])
    lds1 = s_dma.n
    lds1_gate = s_act.n

    evac_marks = []
    silu_marks = {}
    xp_marks = {}
    bcd_marks = []
    nchunk = 0

    def emit_xproj(mi):
        par = mi % 2
        for fi2, (g0, gw) in enumerate(FC2):
            bank = fi2 % 2

            def xpfn(e, mi=mi, g0=g0, gw=gw, bank=bank, par=par):
                return e.matmul(
                    p_bcd[bank][:, :gw],
                    wxpT[:, mi * 48:mi * 48 + 33],
                    xt_t[:, par * 4096 + g0:par * 4096 + g0 + gw],
                    start=True, stop=True)
            w = [(s_act, silu_marks[mi])]
            if len(bcd_marks) >= 2:
                w.append((s_dve, bcd_marks[-2]))
            p.op("tensor", xpfn, waits=w, incs=[(s_pe, 1)])
            pe_now = s_pe.n
            if mi == 0:
                p.op("vector", lambda e, g0=g0, gw=gw, bank=bank:
                     e.tensor_copy(out=bcda[:, g0:g0 + gw],
                                   in_=p_bcd[bank][:, :gw]),
                     waits=[(s_pe, pe_now)], incs=[(s_dve, 1)])
            else:
                p.op("vector", lambda e, g0=g0, gw=gw, bank=bank:
                     e.tensor_tensor(out=bcda[:, g0:g0 + gw],
                                     in0=bcda[:, g0:g0 + gw],
                                     in1=p_bcd[bank][:, :gw], op=OP.add),
                     waits=[(s_pe, pe_now)], incs=[(s_dve, 1)])
            bcd_marks.append(s_dve.n)
        xp_marks[mi] = s_pe.n

    for mi in range(32):
        is_x = mi < 16
        for fi, (f0, fw) in enumerate(FC):
            bank = nchunk % 2
            w = [(s_dma, lds1), (s_act, lds1_gate)]
            if nchunk >= 2:
                w.append((s_dve, evac_marks[nchunk - 2]))

            def mmfn(e, mi=mi, f0=f0, fw=fw, bank=bank):
                ins = None
                for kc in range(8):
                    ins = e.matmul(
                        p_in[bank][:, :fw],
                        winT[:, kc * 4096 + mi * 128:kc * 4096 + (mi + 1) * 128],
                        xT[:, kc * RH + f0:kc * RH + f0 + fw],
                        start=(kc == 0), stop=(kc == 7))
                return ins
            p.op("tensor", mmfn, waits=w, incs=[(s_pe, 1)])
            nchunk += 1
            pe_now = s_pe.n
            if is_x:
                p.op("vector", lambda e, f0=f0, fw=fw, bank=bank: e.tensor_copy(
                    out=xpre[:, f0:f0 + fw], in_=p_in[bank][:, :fw]),
                    waits=[(s_pe, pe_now)], incs=[(s_dve, 1)])
            else:
                p.op("vector", lambda e, f0=f0, fw=fw, bank=bank:
                     e.tensor_copy(
                         out=zbuf[:, f0:f0 + fw],
                         in_=p_in[bank][:, :fw]),
                     waits=[(s_pe, pe_now), (s_dmas, s_dmas.n)],
                     incs=[(s_dve, 1)])
            evac_marks.append(s_dve.n)

        if is_x:
            par = mi % 2
            p.op("vector", lambda e, mi=mi: e.tensor_scalar(
                out=cvA[:, :], in0=xpre[:, 0:4096],
                scalar1=cwt[:, mi * 4:mi * 4 + 1], scalar2=None,
                op0=OP.mult),
                waits=[(s_dve, s_dve.n), (s_dmb, cw_ld)], incs=[(s_dve, 1)])
            cvp = [cvA, cvB, cvA, cvB]
            for j in range(1, 4):
                p.op("vector", lambda e, j=j, mi=mi, src=cvp[j - 1],
                     dst=cvp[j]: e.scalar_tensor_tensor(
                         out=dst[:, :], in0=xpre[:, j:j + 4096],
                         scalar=cwt[:, mi * 4 + j:mi * 4 + j + 1],
                         in1=src[:, :], op0=OP.mult, op1=OP.add),
                     waits=[(s_dve, s_dve.n)], incs=[(s_dve, 1)])
            w = [(s_dve, s_dve.n), (s_dmb, cw_ld)]
            if mi >= 2:
                w.append((s_pe, xp_marks[mi - 2]))
                w.append((s_dmas, s_dmas.n))
            p.op("scalar", lambda e, mi=mi, par=par: e.activation(
                out=xt_t[:, par * 4096:(par + 1) * 4096], in_=cvB[:, :],
                func=AF.Silu, bias=cbt[:, mi:mi + 1]),
                waits=w, incs=[(s_act, 1)])
            silu_marks[mi] = s_act.n
            p.op("gpsimd", lambda e, mi=mi, par=par: e.dma_start(
                out=d_ain[mi * 128:(mi + 1) * 128, :],
                in_=xt_t[:, par * 4096:(par + 1) * 4096]),
                waits=[(s_act, silu_marks[mi])], incs=[(s_dmas, 16)])
            if mi >= 1:
                emit_xproj(mi - 1)
        else:
            if mi == 16:
                emit_xproj(15)
            p.op("gpsimd", lambda e, mi=mi: e.dma_start(
                out=d_z[(mi - 16) * 128:(mi - 15) * 128, :],
                in_=zbuf[:, 3:3 + RPC]),
                waits=[(s_dve, evac_marks[-1])], incs=[(s_dmas, 16)])

    if dbg_tap("wqg", [FI, DM], BF16) is not None:
        p.op("gpsimd", lambda e: e.dma_start(out=dbg["wqg"], in_=d_wqgb),
             waits=[(s_dmas, s_dmas.n)], incs=[(s_out, 16)])
    if dbg_tap("bcd", [33, 4096]) is not None:
        p.op("gpsimd", lambda e: e.dma_start(out=dbg["bcd"], in_=bcda[:, :]),
             waits=[(s_dve, s_dve.n)], incs=[(s_out, 16)])
    if dbg_tap("xtilde", [DI, RPC], BF16) is not None:
        p.op("gpsimd", lambda e: e.dma_start(out=dbg["xtilde"], in_=d_ain),
             waits=[(s_dmas, s_dmas.n)], incs=[(s_out, 16)])
    if dbg_tap("ztap", [DI, RPC], BF16) is not None:
        p.op("gpsimd", lambda e: e.dma_start(out=dbg["ztap"], in_=d_z),
             waits=[(s_dmas, s_dmas.n)], incs=[(s_out, 16)])

    p.op("gpsimd", lambda e: e.dma_start(out=d_bcdo, in_=bcda[:, :]),
         waits=[(s_dve, s_dve.n)], incs=[(s_dmas, 16)])
    p.op("gpsimd", lambda e: e.collective_compute(
        "AllGather", OP.bypass, replica_groups=[list(range(NCORES))],
        ins=[d_bcdo.opt()], outs=[d_bcda.opt()]),
        waits=[(s_dmas, s_dmas.n)], incs=[(s_cc[0], None)])
    def _a2a(e, src, dst):
        with allow_ici_all_to_all():
            return e.collective_compute(
                "AllToAll", OP.bypass, replica_groups=[list(range(NCORES))],
                ins=[src.opt()], outs=[dst.opt()])
    p.op("gpsimd", lambda e: _a2a(e, d_ain, d_aout),
         waits=[(s_dmas, s_dmas.n)], incs=[(s_cc[1], None)])
    p.op("gpsimd", lambda e: e.dma_start(
        out=d_bcdb.rearrange("(r e) f -> r e f", e=32),
        in_=d_bcda.rearrange("(r e) f -> r e f", e=33)[:, 0:32, :]),
        waits=[(s_cc[0], 1)], incs=[(s_dmas, 16)])

    ph1 = [(s_dmas, s_dmas.n), (s_dma, s_dma.n), (s_dve, s_dve.n),
           (s_act, s_act.n), (s_pe, s_pe.n), (s_cc[0], 1), (s_cc[1], 1)]
    phase_end()

    LC = 4096
    xt2 = sbuf("xt2", [128, 2 * LC], BF16)
    dt2 = sbuf("dt2", [128, 2 * LC], BF16)
    u2 = sbuf("u2", [128, 2 * LC], BF16)
    yac = sbuf("yac", [128, 2 * LC], BF16)
    dAa = sbuf("dAa", [128, 2 * LC], BF16)
    scnb = sbuf("scnb", [128, 2 * LC], BF16)
    dBx = scnb[:, 0:LC]
    hh = scnb[:, LC:2 * LC]
    etmp = sbuf("etmp", [128, 2 * LC], BF16)
    brep = sbuf("brep", [128, 2 * LC], BF16)
    crep = sbuf("crep", [128, 2 * LC], BF16)
    dtr = sbuf("dtr", [128, LC], F32)
    tm1 = etmp.ap().bitcast(F32)
    tm2 = scnb.ap().bitcast(F32)
    asc = sbuf("asc", [128, 2 * DS], F32)
    dtw2 = sbuf("dtw2", [128, 2], F32)
    dtb2 = sbuf("dtb2", [128, 2], F32)
    dpar2 = sbuf("dpar2", [128, 2], F32)
    st2 = sbuf("st2", [128, 32], F32)

    p.op("sync", lambda e: e.dma_start(out=dtw2[:, :], in_=dt_w),
         waits=ph1, incs=[(s_dma, 16)])
    p.op("sync", lambda e: e.dma_start(out=dtb2[:, :], in_=dt_b),
         waits=[], incs=[(s_dma, 16)])
    p.op("sync", lambda e: e.dma_start(out=dpar2[:, :], in_=d_par),
         waits=[], incs=[(s_dma, 16)])
    p.op("sync", lambda e: e.dma_start(out=asc[:, :], in_=a_log),
         waits=[], incs=[(s_dma, 16)])
    p.op("scalar", lambda e: e.activation(out=asc[:, :], in_=asc[:, :],
                                          func=AF.Exp),
         waits=[(s_dma, s_dma.n)], incs=[(s_act, 1)])
    p.op("vector", lambda e: e.tensor_single_scalar(
        out=asc[:, :], in_=asc[:, :], scalar=-1.0, op=OP.mult),
        waits=[(s_act, s_act.n)], incs=[(s_dve, 1)])
    asc_done = s_dve.n

    scan_marks = []
    dbx_marks = []
    emul_marks = []
    pool_marks = []
    git = 0
    for b in range(B):
        for lc in range(2):
            blk = 2 * b + lc
            w_head = [(s_dve, s_dve.n if blk == 0 else pre_n),
                      (s_act, 0 if blk == 0 else asc_done and 0)]
            w_head = [x for x in w_head if x[1] > 0]
            if blk == 0:
                w_head = [(s_dve, asc_done), (s_dmas, s_dmas.n)]
            p.op("sync", lambda e, blk=blk: e.dma_start(
                out=xt2[:, 0:LC], in_=d_aout[blk * 256:blk * 256 + 128, :]),
                waits=w_head, incs=[(s_dma, 16)])
            p.op("sync", lambda e, blk=blk: e.dma_start(
                out=xt2[:, LC:2 * LC],
                in_=d_aout[blk * 256 + 128:blk * 256 + 256, :]),
                waits=[], incs=[(s_dma, 16)])
            p.op("sync", lambda e, blk=blk: e.dma_start(
                out=dtr[:, :],
                in_=d_bcda[blk * 33 + 32:blk * 33 + 33, :]
                .partition_broadcast(128)),
                waits=[], incs=[(s_dma, 16)])
            ld2 = s_dma.n
            for dtile in range(2):
                sl = slice(dtile * LC, (dtile + 1) * LC)
                p.op("vector", lambda e, dtile=dtile: e.tensor_scalar(
                    out=tm1, in0=dtr[:, :],
                    scalar1=dtw2[:, dtile:dtile + 1],
                    scalar2=dtb2[:, dtile:dtile + 1],
                    op0=OP.mult, op1=OP.add),
                    waits=[(s_dma, ld2), (s_act, s_act.n)],
                    incs=[(s_dve, 1)])
                p.op("scalar", lambda e: e.activation(
                    out=tm2, in_=tm1, func=AF.Abs),
                    waits=[(s_dve, s_dve.n)], incs=[(s_act, 1)])
                p.op("scalar", lambda e: e.activation(
                    out=tm2, in_=tm2, func=AF.Exp, scale=-1.0),
                    waits=[(s_act, s_act.n)], incs=[(s_act, 1)])
                p.op("scalar", lambda e: e.activation(
                    out=tm2, in_=tm2, func=AF.Ln, bias=1.0),
                    waits=[(s_act, s_act.n)], incs=[(s_act, 1)])
                p.op("vector", lambda e: e.tensor_single_scalar(
                    out=tm1, in_=tm1, scalar=0.0, op=OP.max),
                    waits=[(s_dve, s_dve.n)], incs=[(s_dve, 1)])
                p.op("vector", lambda e, sl=sl: e.tensor_tensor(
                    out=dt2[:, sl], in0=tm1, in1=tm2, op=OP.add),
                    waits=[(s_dve, s_dve.n), (s_act, s_act.n)],
                    incs=[(s_dve, 1)])
                p.op("vector", lambda e, sl=sl: e.tensor_tensor(
                    out=u2[:, sl], in0=dt2[:, sl], in1=xt2[:, sl],
                    op=OP.mult),
                    waits=[(s_dve, s_dve.n)], incs=[(s_dve, 1)])
                p.op("vector", lambda e, sl=sl, dtile=dtile: e.tensor_scalar(
                    out=yac[:, sl], in0=xt2[:, sl],
                    scalar1=dpar2[:, dtile:dtile + 1], scalar2=None,
                    op0=OP.mult),
                    waits=[(s_dve, s_dve.n), (s_dmas, s_dmas.n)],
                    incs=[(s_dve, 1)])
            pre_n = s_dve.n

            for n in range(DS):
                npar = n % 2
                bsl = slice(npar * LC, (npar + 1) * LC)
                wb = []
                if len(dbx_marks) >= 3:
                    wb.append((s_dve, dbx_marks[-3]))
                p.op("sync", lambda e, blk=blk, n=n, bsl=bsl: e.dma_start(
                    out=brep[:, bsl],
                    in_=d_bcdb[blk * 32 + n:blk * 32 + n + 1, :]
                    .partition_broadcast(128)),
                    waits=wb, incs=[(s_dma, 16)])
                wc = []
                if len(emul_marks) >= 3:
                    wc.append((s_dve, emul_marks[-3]))
                p.op("scalar", lambda e, blk=blk, n=n, bsl=bsl: e.dma_start(
                    out=crep[:, bsl],
                    in_=d_bcdb[blk * 32 + 16 + n:blk * 32 + 17 + n, :]
                    .partition_broadcast(128)),
                    waits=wc, incs=[(s_dmb, 16)])
                bc_ld = s_dma.n
                bc_ldb = s_dmb.n
                for dtile in range(2):
                    sl = slice(dtile * LC, (dtile + 1) * LC)
                    apar = git % 2
                    asl = slice(apar * LC, (apar + 1) * LC)
                    aw = [(s_dve, pre_n)]
                    if git >= 2:
                        aw.append((s_dve, scan_marks[git - 2]))
                    p.op("scalar", lambda e, sl=sl, asl=asl, dtile=dtile, n=n:
                         e.activation(
                             out=dAa[:, asl], in_=dt2[:, sl], func=AF.Exp,
                             scale=asc[:, dtile * DS + n:dtile * DS + n + 1]),
                         waits=aw, incs=[(s_act, 1)])
                    act_n = s_act.n
                    p.op("vector", lambda e, sl=sl, bsl=bsl: e.tensor_tensor(
                        out=dBx, in0=u2[:, sl], in1=brep[:, bsl],
                        op=OP.mult),
                        waits=[(s_dma, bc_ld), (s_dmb, bc_ldb),
                               (s_dve, s_dve.n)],
                        incs=[(s_dve, 1)])
                    dbx_marks.append(s_dve.n)
                    sidx = 2 * n + dtile
                    if lc == 0:
                        init = 0.0
                    else:
                        init = st2[:, sidx:sidx + 1]
                    p.op("vector", lambda e, asl=asl, init=init:
                         e.tensor_tensor_scan(
                             out=hh,
                             data0=dAa[:, asl], data1=dBx,
                             initial=init, op0=OP.mult, op1=OP.add),
                         waits=[(s_act, act_n), (s_dve, s_dve.n)],
                         incs=[(s_dve, 1)])
                    scan_marks.append(s_dve.n)
                    if lc == 0:
                        p.op("vector", lambda e, sidx=sidx: e.tensor_copy(
                            out=st2[:, sidx:sidx + 1], in_=hh[:, LC - 1:LC]),
                            waits=[(s_dve, s_dve.n)], incs=[(s_dve, 1)])
                    ew = [(s_dve, s_dve.n)]
                    p.op("vector", lambda e, bsl=bsl, epar=git % 2:
                         e.tensor_tensor(
                             out=etmp[:, epar * LC:(epar + 1) * LC],
                             in0=hh, in1=crep[:, bsl], op=OP.mult),
                         waits=ew, incs=[(s_dve, 1)])
                    emul_marks.append(s_dve.n)
                    p.op("vector", lambda e, sl=sl, epar=git % 2:
                         e.tensor_tensor(
                             out=yac[:, sl], in0=yac[:, sl],
                             in1=etmp[:, epar * LC:(epar + 1) * LC],
                             op=OP.add),
                         waits=[(s_dve, emul_marks[-1])],
                         incs=[(s_dve, 1)])
                    pool_marks.append(s_dve.n)
                    git += 1
            for dtile in range(2):
                sl = slice(dtile * LC, (dtile + 1) * LC)
                p.op("gpsimd", lambda e, blk=blk, dtile=dtile, sl=sl:
                     e.dma_start(
                         out=d_yin[blk * 256 + dtile * 128:
                                   blk * 256 + (dtile + 1) * 128, :],
                         in_=yac[:, sl]),
                     waits=[(s_dve, s_dve.n)], incs=[(s_dmas, 16)])

    p.op("gpsimd", lambda e: _a2a(e, d_yin, d_yout),
         waits=[(s_dmas, s_dmas.n)], incs=[(s_cc[2], None)])

    if dbg_tap("ytap", [DI, RPC], BF16) is not None:
        p.op("gpsimd", lambda e: e.dma_start(out=dbg["ytap"], in_=d_yin),
             waits=[(s_dmas, s_dmas.n)], incs=[(s_out, 16)])

    ph2 = [(s_dmas, s_dmas.n), (s_dma, s_dma.n), (s_dve, s_dve.n),
           (s_act, s_act.n), (s_cc[2], 1)]
    phase_end()

    gbuf = sbuf("gbuf", [128, 16 * 2048], BF16)
    woT = sbuf("woT", [128, 16 * DM], BF16)
    ytl = sbuf("ytl", [128, 2 * 2048], BF16)
    ztl = sbuf("ztl", [128, 2 * 2048], BF16)
    szl = sbuf("szl", [128, 2 * 2048], BF16)
    a1t = sbuf("a1t", [128, 2 * DM], F32)
    a2t = sbuf("a2t", [128, 2 * DM], F32)
    xrow = sbuf("xrow", [128, 2 * DM], BF16)
    x1t = sbuf("x1t", [128, 2 * DM], BF16)
    x1f = sbuf("x1f", [128, 2 * DM], F32)
    lnt = sbuf("lnt", [128, 16], F32)
    qo = sbuf("qo", [128, DM], BF16)
    epsA = sbuf("epsA", [128, 1], F32)
    QTA = mk_qt("a", DM)
    p_op = [[psum("p_op00", [128, 512]), psum("p_op01", [128, 512])],
            [psum("p_op10", [128, 512]), psum("p_op11", [128, 512])]]

    for kc in range(16):
        p.op("sync", lambda e, kc=kc: e.dma_start(
            out=woT[:, kc * DM:(kc + 1) * DM],
            in_=d_woutb[:, kc * 128:(kc + 1) * 128], transpose=True),
            waits=ph2, incs=[(s_wo, 16)])
    wo_ld = s_wo.n
    p.op("scalar", lambda e: e.activation(
        out=dly[:, :], in_=dly[:, :], func=AF.Square),
        waits=[(s_wo, wo_ld)], incs=[(s_act, 1)])
    wo_gate = s_act.n
    p.op("vector", lambda e: e.memset(epsA[:, :], EPS), waits=ph2,
         incs=[(s_dve, 1)])

    res_marks = {}
    xrow_marks = {}
    qo_marks = {}
    x1dma_marks = {}
    g_done_at = {}

    def a_mm(tt):
        half, ttl, par = tt // 16, tt % 16, tt % 2
        for fh in range(2):
            def opfn(e, ttl=ttl, fh=fh, par=par, half=half):
                ins = None
                for kc in range(16):
                    ins = e.matmul(
                        p_op[par][fh][:, :],
                        gbuf[:, kc * 2048 + ttl * 128:
                             kc * 2048 + (ttl + 1) * 128],
                        woT[:, kc * DM + fh * 512:kc * DM + (fh + 1) * 512],
                        start=(kc == 0), stop=(kc == 15))
                return ins
            w = [(s_dve, g_done_at[half]), (s_wo, wo_ld),
                 (s_act, wo_gate)]
            if tt >= 2:
                w.append((s_dve, res_marks[tt - 2]))
            p.op("tensor", opfn, waits=w, incs=[(s_pe, 1)])
        res_marks[(tt, "pe")] = s_pe.n
        w = [(s_dve, xrow_marks.get(tt - 2, 0))]
        psl = slice(par * DM, (par + 1) * DM)
        p.op("sync", lambda e, tt=tt, psl=psl: e.dma_start(
            out=xrow[:, psl], in_=d_xb[3 + tt * 128:3 + (tt + 1) * 128, :]),
            waits=[x for x in w if x[1] > 0], incs=[(s_dma, 16)])
        res_marks[(tt, "xd")] = s_dma.n

    def a_s1(tt):
        par = tt % 2
        psl = slice(par * DM, (par + 1) * DM)
        p.op("vector", lambda e, psl=psl: e.tensor_copy(
            out=a2t[:, psl], in_=xrow[:, psl]),
            waits=[(s_dma, res_marks[(tt, "xd")]), (s_act, s_act.n)],
            incs=[(s_dve, 1)])
        xrow_marks[tt] = s_dve.n
        p.op("vector", lambda e, par=par: e.tensor_tensor(
            out=a1t[:, par * DM:par * DM + 512],
            in0=a2t[:, par * DM:par * DM + 512],
            in1=p_op[par][0][:, :], op=OP.add),
            waits=[(s_pe, res_marks[(tt, "pe")]), (s_dve, s_dve.n)],
            incs=[(s_dve, 1)])
        p.op("vector", lambda e, par=par: e.tensor_tensor(
            out=a1t[:, par * DM + 512:par * DM + 1024],
            in0=a2t[:, par * DM + 512:par * DM + 1024],
            in1=p_op[par][1][:, :], op=OP.add),
            waits=[(s_dve, s_dve.n)], incs=[(s_dve, 1)])
        res_marks[tt] = s_dve.n
        p.op("vector", lambda e, par=par, psl=psl: e.reduce_sum(
            out=lnt[:, par * 8:par * 8 + 1], in_=a1t[:, psl],
            axis=mybir.AxisListType.X),
            waits=[(s_dve, s_dve.n)], incs=[(s_dve, 1)])
        p.op("vector", lambda e, par=par: e.tensor_single_scalar(
            out=lnt[:, par * 8:par * 8 + 1],
            in_=lnt[:, par * 8:par * 8 + 1], scalar=1.0 / DM,
            op=OP.mult), waits=[(s_dve, s_dve.n)], incs=[(s_dve, 1)])
        p.op("scalar", lambda e, psl=psl: e.activation(
            out=a2t[:, psl], in_=a1t[:, psl], func=AF.Square),
            waits=[(s_dve, s_dve.n)], incs=[(s_act, 1)])
        res_marks[(tt, "sq")] = s_act.n

    def a_s2(tt):
        par = tt % 2
        psl = slice(par * DM, (par + 1) * DM)
        p.op("vector", lambda e, par=par, psl=psl: e.reduce_sum(
            out=lnt[:, par * 8 + 1:par * 8 + 2], in_=a2t[:, psl],
            axis=mybir.AxisListType.X),
            waits=[(s_act, res_marks[(tt, "sq")])], incs=[(s_dve, 1)])
        p.op("vector", lambda e, par=par: e.tensor_scalar(
            out=lnt[:, par * 8 + 2:par * 8 + 3],
            in0=lnt[:, par * 8:par * 8 + 1],
            scalar1=lnt[:, par * 8:par * 8 + 1], scalar2=None,
            op0=OP.mult),
            waits=[(s_dve, s_dve.n)], incs=[(s_dve, 1)])
        p.op("vector", lambda e, par=par: e.tensor_scalar(
            out=lnt[:, par * 8 + 3:par * 8 + 4],
            in0=lnt[:, par * 8 + 1:par * 8 + 2], scalar1=1.0 / DM,
            scalar2=lnt[:, par * 8 + 2:par * 8 + 3], op0=OP.mult,
            op1=OP.subtract),
            waits=[(s_dve, s_dve.n)], incs=[(s_dve, 1)])
        p.op("scalar", lambda e, par=par: e.activation(
            out=lnt[:, par * 8 + 4:par * 8 + 5],
            in_=lnt[:, par * 8 + 3:par * 8 + 4], func=AF.Ln,
            bias=epsA[:, 0:1]),
            waits=[(s_dve, s_dve.n)], incs=[(s_act, 1)])
        p.op("scalar", lambda e, par=par: e.activation(
            out=lnt[:, par * 8 + 5:par * 8 + 6],
            in_=lnt[:, par * 8 + 4:par * 8 + 5], func=AF.Exp, scale=-0.5),
            waits=[(s_act, s_act.n)], incs=[(s_act, 1)])
        res_marks[(tt, "rstd")] = s_act.n

    def a_s3(tt):
        par = tt % 2
        psl = slice(par * DM, (par + 1) * DM)
        p.op("vector", lambda e, par=par, psl=psl: e.tensor_scalar(
            out=x1f[:, psl], in0=a1t[:, psl],
            scalar1=lnt[:, par * 8:par * 8 + 1],
            scalar2=lnt[:, par * 8 + 5:par * 8 + 6],
            op0=OP.subtract, op1=OP.mult),
            waits=[(s_act, res_marks[(tt, "rstd")]), (s_dve, s_dve.n),
                   (s_dmas, s_dmas.n)],
            incs=[(s_dve, 1)])
        wx1 = [(s_dve, s_dve.n)]
        if tt >= 2:
            wx1.append((s_dmas, x1dma_marks[tt - 2]))
        p.op("vector", lambda e, psl=psl: e.tensor_copy(
            out=x1t[:, psl], in_=x1f[:, psl]),
            waits=wx1, incs=[(s_dve, 1)])
        res_marks[(tt, "x1")] = s_dve.n
        p.op("gpsimd", lambda e, tt=tt, psl=psl: e.dma_start(
            out=d_x1[tt * 128:(tt + 1) * 128, :], in_=x1t[:, psl]),
            waits=[(s_dve, res_marks[(tt, "x1")])], incs=[(s_dmas, 16)])
        x1dma_marks[tt] = s_dmas.n

    def a_q(tt):
        par = tt % 2
        psl = slice(par * DM, (par + 1) * DM)
        wq = [(s_dve, res_marks[(tt, "x1")])]
        if tt >= 1:
            wq.append((s_dmas, qo_marks[tt - 1]))
        emit_quant(128, DM, x1f[:, psl], qo[:, :], QTA, wq)
        p.op("gpsimd", lambda e, tt=tt: e.dma_start(
            out=d_xq[tt * 128:(tt + 1) * 128, :], in_=qo[:, :]),
            waits=[(s_dve, s_dve.n)], incs=[(s_dmas, 16)])
        qo_marks[tt] = s_dmas.n

    gtt = {}
    silu_g = {}
    yld = {}
    pe_half = {}
    for half in range(2):
        t0 = half * 2048
        for kc in range(16):
            gkc = half * 16 + kc
            q = gkc % 2
            qsl = slice(q * 2048, (q + 1) * 2048)
            wy = [(s_wo, wo_ld)]
            if gkc >= 2:
                wy.append((s_dve, gtt[gkc - 2]))
                wy.append((s_act, silu_g[gkc - 2]))
            p.op("sync", lambda e, kc=kc, t0=t0, qsl=qsl: e.dma_start(
                out=ytl[:, qsl],
                in_=d_yout[kc * 128:(kc + 1) * 128, t0:t0 + 2048]),
                waits=wy, incs=[(s_dma, 16)])
            p.op("sync", lambda e, kc=kc, t0=t0, qsl=qsl: e.dma_start(
                out=ztl[:, qsl],
                in_=d_z[kc * 128:(kc + 1) * 128, t0:t0 + 2048]),
                waits=[], incs=[(s_dma, 16)])
            yld[gkc] = s_dma.n
            ws = [(s_dma, yld[gkc])]
            if gkc >= 2:
                ws.append((s_dve, gtt[gkc - 2]))
            p.op("scalar", lambda e, qsl=qsl: e.activation(
                out=szl[:, qsl], in_=ztl[:, qsl], func=AF.Silu),
                waits=ws, incs=[(s_act, 1)])
            silu_g[gkc] = s_act.n
            wt = [(s_act, silu_g[gkc]), (s_dma, yld[gkc])]
            if half == 1:
                wt.append((s_pe, pe_half[0]))
            p.op("vector", lambda e, kc=kc, qsl=qsl: e.tensor_tensor(
                out=gbuf[:, kc * 2048:(kc + 1) * 2048], in0=ytl[:, qsl],
                in1=szl[:, qsl], op=OP.mult),
                waits=wt, incs=[(s_dve, 1)])
            gtt[gkc] = s_dve.n
        g_done_at[half] = s_dve.n
        base = half * 16
        for i in range(16):
            tt = base + i
            a_mm(tt)
            a_s1(tt)
            if i >= 1:
                a_s3(tt - 1)
                a_q(tt - 1)
            a_s2(tt)
        a_s3(base + 15)
        a_q(base + 15)
        pe_half[half] = s_pe.n

    if dbg_tap("x1tap", [RPC, DM], BF16) is not None:
        p.op("gpsimd", lambda e: e.dma_start(out=dbg["x1tap"], in_=d_x1),
             waits=[(s_dmas, s_dmas.n)], incs=[(s_out, 16)])
    if dbg_tap("xqtap", [RPC, DM], BF16) is not None:
        p.op("gpsimd", lambda e: e.dma_start(out=dbg["xqtap"], in_=d_xq),
             waits=[(s_dmas, s_dmas.n)], incs=[(s_out, 16)])
    pa = [(s_dmas, s_dmas.n), (s_dma, s_dma.n), (s_dve, s_dve.n),
          (s_act, s_act.n), (s_pe, s_pe.n)]
    phase_end()

    xqT = sbuf("xqT", [128, 8 * RPC], BF16)
    wgT = sbuf("wgT", [128, 8 * FI], BF16)
    wuT = sbuf("wuT", [128, 8 * FI], BF16)
    sgf = sbuf("sgf", [128, 512], F32)
    htf = sbuf("htf", [128, 2 * FI], BF16)
    htq = sbuf("htq", [128, FI], BF16)
    QTB = mk_qt2("b", FI)
    p_g = [psum("p_g0", [128, 512]), psum("p_g1", [128, 512])]
    p_u = [psum("p_u0", [128, 512]), psum("p_u1", [128, 512])]

    for kc in range(8):
        p.op("sync", lambda e, kc=kc: e.dma_start(
            out=xqT[:, kc * RPC:(kc + 1) * RPC],
            in_=d_xq[:, kc * 128:(kc + 1) * 128], transpose=True),
            waits=pa, incs=[(s_dma, 16)])
        p.op("sync", lambda e, kc=kc: e.dma_start(
            out=wgT[:, kc * FI:(kc + 1) * FI],
            in_=d_wqgb[:, kc * 128:(kc + 1) * 128], transpose=True),
            waits=pa, incs=[(s_dma, 16)])
        p.op("sync", lambda e, kc=kc: e.dma_start(
            out=wuT[:, kc * FI:(kc + 1) * FI],
            in_=d_wqub[:, kc * 128:(kc + 1) * 128], transpose=True),
            waits=pa, incs=[(s_dma, 16)])
    pb_ld = s_dma.n
    p.op("scalar", lambda e: e.activation(
        out=dly[:, :], in_=dly[:, :], func=AF.Square),
        waits=[(s_dma, pb_ld)], incs=[(s_act, 1)])
    pb_gate = s_act.n

    NCH = [(i * 512, 512) for i in range(5)] + [(2560, 256)]
    silu_b = {}
    httm = {}
    httm_t = {}
    hq_dma = {}
    habs = {}
    qfin = {}
    cig_box = [0]

    def b_chunks(tt):
        par = tt % 2
        for ci, (n0, nw) in enumerate(NCH):
            cig = cig_box[0]
            bank = cig % 2

            def gfn(e, tt=tt, n0=n0, nw=nw, bank=bank):
                ins = None
                for kc in range(8):
                    ins = e.matmul(
                        p_g[bank][:, :nw],
                        xqT[:, kc * RPC + tt * 128:kc * RPC + (tt + 1) * 128],
                        wgT[:, kc * FI + n0:kc * FI + n0 + nw],
                        start=(kc == 0), stop=(kc == 7))
                return ins

            def ufn(e, tt=tt, n0=n0, nw=nw, bank=bank):
                ins = None
                for kc in range(8):
                    ins = e.matmul(
                        p_u[bank][:, :nw],
                        xqT[:, kc * RPC + tt * 128:kc * RPC + (tt + 1) * 128],
                        wuT[:, kc * FI + n0:kc * FI + n0 + nw],
                        start=(kc == 0), stop=(kc == 7))
                return ins
            wg_ = [(s_dma, pb_ld), (s_act, pb_gate)]
            if cig >= 2:
                wg_.append((s_act, silu_b[cig - 2]))
            p.op("tensor", gfn, waits=wg_, incs=[(s_pe, 1)])
            wu_ = []
            if cig >= 2:
                wu_.append((s_dve, httm[cig - 2]))
            p.op("tensor", ufn, waits=wu_, incs=[(s_pe, 1)])
            pe_now = s_pe.n
            wsil = [(s_pe, pe_now - 1)]
            if cig >= 1:
                wsil.append((s_dve, httm[cig - 1]))
            p.op("scalar", lambda e, nw=nw, bank=bank: e.activation(
                out=sgf[:, :nw], in_=p_g[bank][:, :nw],
                func=AF.Silu),
                waits=wsil, incs=[(s_act, 1)])
            silu_b[cig] = s_act.n
            wht = [(s_act, silu_b[cig]), (s_pe, pe_now), (s_dve, s_dve.n)]
            p.op("vector", lambda e, nw=nw, bank=bank, n0=n0, par=par:
                 e.tensor_tensor(
                     out=htf[:, par * FI + n0:par * FI + n0 + nw],
                     in0=sgf[:, :nw], in1=p_u[bank][:, :nw], op=OP.mult),
                 waits=wht, incs=[(s_dve, 1)])
            httm[cig] = s_dve.n
            cig_box[0] = cig + 1
        httm_t[tt] = s_dve.n

    def b_quant_head(pv):
        pvpar = pv % 2
        w = [(s_dve, max(httm_t[pv], qfin.get(pv - 1, 0)))]
        habs[pv] = emit_quant2_head(
            128, FI, htf[:, pvpar * FI:(pvpar + 1) * FI], QTB, w)

    def b_quant_rest(pv):
        wr = []
        if pv >= 1:
            wr.append((s_dmas, hq_dma[pv - 1]))
        qfin[pv] = emit_quant2_rest(128, FI, htq[:, :], QTB, habs[pv], wr)
        p.op("gpsimd", lambda e, pv=pv: e.dma_start(
            out=d_hq[pv * 128:(pv + 1) * 128, :], in_=htq[:, :]),
            waits=[(s_dve, qfin[pv])], incs=[(s_dmas, 16)])
        hq_dma[pv] = s_dmas.n

    for tt in range(32):
        if tt >= 1:
            b_quant_head(tt - 1)
        b_chunks(tt)
        if tt >= 1:
            b_quant_rest(tt - 1)
    b_quant_head(31)
    b_quant_rest(31)

    if dbg_tap("hqtap", [RPC, FI], BF16) is not None:
        p.op("gpsimd", lambda e: e.dma_start(out=dbg["hqtap"], in_=d_hq),
             waits=[(s_dmas, s_dmas.n)], incs=[(s_out, 16)])
    pb = [(s_dmas, s_dmas.n), (s_dma, s_dma.n), (s_dve, s_dve.n),
          (s_act, s_act.n), (s_pe, s_pe.n)]
    phase_end()

    hqT = sbuf("hqT", [128, 2 * 22 * 512], BF16)
    wdT = sbuf("wdT", [128, 22 * DM], BF16)
    c_a1 = sbuf("c_a1", [128, 2 * DM], F32)
    c_a2 = sbuf("c_a2", [128, 2 * DM], F32)
    c_x1 = sbuf("c_x1", [128, 2 * DM], BF16)
    c_ln = sbuf("c_ln", [128, 16], F32)
    c_o = sbuf("c_o", [128, 2 * DM], F32)
    epsC = sbuf("epsC", [128, 1], F32)
    p_d = [[psum("p_d00", [128, 512]), psum("p_d01", [128, 512])],
           [psum("p_d10", [128, 512]), psum("p_d11", [128, 512])]]

    for kc in range(22):
        p.op("sync", lambda e, kc=kc: e.dma_start(
            out=wdT[:, kc * DM:(kc + 1) * DM],
            in_=d_wqdb[:, kc * 128:(kc + 1) * 128], transpose=True),
            waits=pb, incs=[(s_dma, 16)])
    wd_ld = s_dma.n
    p.op("vector", lambda e: e.memset(epsC[:, :], EPS), waits=pb,
         incs=[(s_dve, 1)])

    pe_sup = {}
    cres = {}
    cxm = {}
    cx_n = {}
    mm_m = {}
    sq_m = {}
    rstd_m = {}
    outm = {}
    hq_lds = {}
    hq_gates = {}

    def c_loads(sup):
        spar = sup % 2
        st0 = sup * 512
        w = [(s_dma, wd_ld)]
        if sup >= 2:
            w.append((s_pe, pe_sup[sup - 2]))
        for kc in range(22):
            p.op("sync", lambda e, kc=kc, st0=st0, spar=spar: e.dma_start(
                out=hqT[:, spar * 22 * 512 + kc * 512:
                        spar * 22 * 512 + (kc + 1) * 512],
                in_=d_hq[st0:st0 + 512, kc * 128:(kc + 1) * 128],
                transpose=True),
                waits=(w if kc == 0 else []),
                incs=[(s_hq, 16)])
        hq_lds[sup] = s_hq.n
        p.op("scalar", lambda e: e.activation(
            out=dly[:, :], in_=dly[:, :], func=AF.Square),
            waits=[(s_hq, hq_lds[sup])], incs=[(s_act, 1)])
        hq_gates[sup] = s_act.n

    def c_mm(tt):
        sup, ttl, par = tt // 4, tt % 4, tt % 2
        spar = sup % 2
        psl = slice(par * DM, (par + 1) * DM)
        for fh in range(2):
            def dfn(e, ttl=ttl, fh=fh, par=par, spar=spar):
                ins = None
                for kc in range(22):
                    ins = e.matmul(
                        p_d[par][fh][:, :],
                        hqT[:, spar * 22 * 512 + kc * 512 + ttl * 128:
                            spar * 22 * 512 + kc * 512 + (ttl + 1) * 128],
                        wdT[:, kc * DM + fh * 512:kc * DM + (fh + 1) * 512],
                        start=(kc == 0), stop=(kc == 21))
                return ins
            w2 = [(s_hq, hq_lds[sup]), (s_act, hq_gates[sup])]
            if tt >= 2:
                w2.append((s_dve, cres[tt - 2]))
            p.op("tensor", dfn, waits=w2, incs=[(s_pe, 1)])
        mm_m[tt] = s_pe.n
        pe_sup[sup] = s_pe.n
        p.op("scalar", lambda e, tt=tt, psl=psl: e.dma_start(
            out=c_x1[:, psl], in_=d_x1[tt * 128:(tt + 1) * 128, :]),
            waits=([(s_dve, cxm[tt - 2])] if tt >= 2 else []),
            incs=[(s_cx, 16)])
        cx_n[tt] = s_cx.n

    def c_s1(tt):
        par = tt % 2
        psl = slice(par * DM, (par + 1) * DM)
        p.op("vector", lambda e, psl=psl: e.tensor_copy(
            out=c_a2[:, psl], in_=c_x1[:, psl]),
            waits=[(s_cx, cx_n[tt])], incs=[(s_dve, 1)])
        cxm[tt] = s_dve.n
        p.op("vector", lambda e, par=par: e.tensor_tensor(
            out=c_a1[:, par * DM:par * DM + 512],
            in0=c_a2[:, par * DM:par * DM + 512], in1=p_d[par][0][:, :],
            op=OP.add), waits=[(s_pe, mm_m[tt]), (s_dve, s_dve.n)],
            incs=[(s_dve, 1)])
        p.op("vector", lambda e, par=par: e.tensor_tensor(
            out=c_a1[:, par * DM + 512:par * DM + 1024],
            in0=c_a2[:, par * DM + 512:par * DM + 1024],
            in1=p_d[par][1][:, :], op=OP.add),
            waits=[(s_dve, s_dve.n)], incs=[(s_dve, 1)])
        cres[tt] = s_dve.n
        p.op("vector", lambda e, par=par, psl=psl: e.reduce_sum(
            out=c_ln[:, par * 8:par * 8 + 1], in_=c_a1[:, psl],
            axis=mybir.AxisListType.X),
            waits=[(s_dve, s_dve.n)], incs=[(s_dve, 1)])
        p.op("vector", lambda e, par=par: e.tensor_single_scalar(
            out=c_ln[:, par * 8:par * 8 + 1],
            in_=c_ln[:, par * 8:par * 8 + 1], scalar=1.0 / DM,
            op=OP.mult), waits=[(s_dve, s_dve.n)], incs=[(s_dve, 1)])
        p.op("scalar", lambda e, psl=psl: e.activation(
            out=c_a2[:, psl], in_=c_a1[:, psl], func=AF.Square),
            waits=[(s_dve, s_dve.n)], incs=[(s_act, 1)])
        sq_m[tt] = s_act.n

    def c_s2(tt):
        par = tt % 2
        psl = slice(par * DM, (par + 1) * DM)
        p.op("vector", lambda e, par=par, psl=psl: e.reduce_sum(
            out=c_ln[:, par * 8 + 1:par * 8 + 2], in_=c_a2[:, psl],
            axis=mybir.AxisListType.X),
            waits=[(s_act, sq_m[tt])], incs=[(s_dve, 1)])
        p.op("vector", lambda e, par=par: e.tensor_scalar(
            out=c_ln[:, par * 8 + 2:par * 8 + 3],
            in0=c_ln[:, par * 8:par * 8 + 1],
            scalar1=c_ln[:, par * 8:par * 8 + 1], scalar2=None,
            op0=OP.mult),
            waits=[(s_dve, s_dve.n)], incs=[(s_dve, 1)])
        p.op("vector", lambda e, par=par: e.tensor_scalar(
            out=c_ln[:, par * 8 + 3:par * 8 + 4],
            in0=c_ln[:, par * 8 + 1:par * 8 + 2], scalar1=1.0 / DM,
            scalar2=c_ln[:, par * 8 + 2:par * 8 + 3], op0=OP.mult,
            op1=OP.subtract),
            waits=[(s_dve, s_dve.n)], incs=[(s_dve, 1)])
        p.op("scalar", lambda e, par=par: e.activation(
            out=c_ln[:, par * 8 + 4:par * 8 + 5],
            in_=c_ln[:, par * 8 + 3:par * 8 + 4], func=AF.Ln,
            bias=epsC[:, 0:1]),
            waits=[(s_dve, s_dve.n)], incs=[(s_act, 1)])
        p.op("scalar", lambda e, par=par: e.activation(
            out=c_ln[:, par * 8 + 5:par * 8 + 6],
            in_=c_ln[:, par * 8 + 4:par * 8 + 5], func=AF.Exp,
            scale=-0.5),
            waits=[(s_act, s_act.n)], incs=[(s_act, 1)])
        rstd_m[tt] = s_act.n

    def c_s3(tt):
        par = tt % 2
        psl = slice(par * DM, (par + 1) * DM)
        wo2 = [(s_act, rstd_m[tt]), (s_dve, s_dve.n)]
        if tt >= 2:
            wo2.append((s_out, outm[tt - 2]))
        p.op("vector", lambda e, par=par, psl=psl: e.tensor_scalar(
            out=c_o[:, psl], in0=c_a1[:, psl],
            scalar1=c_ln[:, par * 8:par * 8 + 1],
            scalar2=c_ln[:, par * 8 + 5:par * 8 + 6],
            op0=OP.subtract, op1=OP.mult),
            waits=wo2, incs=[(s_dve, 1)])
        p.op("gpsimd", lambda e, tt=tt, psl=psl: e.dma_start(
            out=out[tt * 128:(tt + 1) * 128, :], in_=c_o[:, psl]),
            waits=[(s_dve, s_dve.n)], incs=[(s_out, 16)])
        outm[tt] = s_out.n

    for tt in range(32):
        if tt % 4 == 0:
            c_loads(tt // 4)
        c_mm(tt)
        c_s1(tt)
        if tt >= 1:
            c_s3(tt - 1)
        c_s2(tt)
    c_s3(31)

    final = s_out.n
    p.op("gpsimd", lambda e: e.engine_nop(), waits=[(s_out, final)])

    phase_end()
    with nc.Block() as block:
        p.emit(block)
    stack.close()
    return nc, dbg


def _host_prep(inputs):
    x = np.ascontiguousarray(np.asarray(inputs["L0_attn_x"], np.float32))
    xr = x.reshape(B * L, DM)
    w_in = np.ascontiguousarray(np.asarray(inputs["L0_attn_in_proj_w"], np.float32))
    conv_w = np.ascontiguousarray(
        np.asarray(inputs["L0_attn_conv_w"], np.float32)
        .reshape(16, 128, DC).transpose(1, 0, 2).reshape(128, 16 * DC))
    conv_b = np.ascontiguousarray(
        np.asarray(inputs["L0_attn_conv_b"], np.float32)
        .reshape(16, 128).transpose(1, 0))
    w_xp = np.ascontiguousarray(np.asarray(inputs["L0_attn_x_proj_w"], np.float32))
    dt_w_full = np.asarray(inputs["L0_attn_dt_proj_w"], np.float32)[:, 0]
    dt_b_full = np.asarray(inputs["L0_attn_dt_proj_b"], np.float32)
    a_log_full = np.asarray(inputs["L0_attn_A_log"], np.float32)
    d_par_full = np.asarray(inputs["L0_attn_D_param"], np.float32)
    w_out = np.ascontiguousarray(np.asarray(inputs["L0_attn_out_proj_w"], np.float32))
    w_g = np.ascontiguousarray(np.asarray(inputs["L0_ffn_w_gate"], np.float32))
    w_u = np.ascontiguousarray(np.asarray(inputs["L0_ffn_w_up"], np.float32))
    w_d = np.ascontiguousarray(np.asarray(inputs["L0_ffn_w_down"], np.float32))

    in_maps = []
    for j in range(NCORES):
        r0 = j * RPC
        if j % 2 == 0:
            halo = np.zeros((3, DM), np.float32)
        else:
            halo = xr[r0 - 3:r0]
        tailpad = np.zeros((RH - 3 - RPC, DM), np.float32)
        x_rows = np.ascontiguousarray(
            np.concatenate([halo, xr[r0:r0 + RPC], tailpad], 0))
        ds0 = j * DPC
        in_maps.append({
            "x_rows": x_rows,
            "w_in": w_in,
            "conv_w": conv_w,
            "conv_b": conv_b,
            "w_xp": w_xp,
            "dt_w": np.ascontiguousarray(
                dt_w_full[ds0:ds0 + DPC].reshape(2, 128).transpose(1, 0)),
            "dt_b": np.ascontiguousarray(
                dt_b_full[ds0:ds0 + DPC].reshape(2, 128).transpose(1, 0)),
            "a_log": np.ascontiguousarray(
                a_log_full[ds0:ds0 + DPC].reshape(2, 128, DS)
                .transpose(1, 0, 2).reshape(128, 2 * DS)),
            "d_par": np.ascontiguousarray(
                d_par_full[ds0:ds0 + DPC].reshape(2, 128).transpose(1, 0)),
            "w_out": w_out,
            "w_g": w_g,
            "w_u": w_u,
            "w_d": w_d,
        })
    return in_maps


def kernel(**inputs):
    in_maps = _host_prep(inputs)
    nc, _ = build_program()
    res = run_bass_kernel_spmd(nc, in_maps, core_ids=list(range(NCORES)))
    outs = [np.asarray(res.results[j]["out"], np.float32) for j in range(NCORES)]
    return np.concatenate(outs, 0).reshape(B, L, DM)


# version 26
# speedup vs baseline: 1.0638x; 1.0638x over previous
import numpy as np
import concourse.bass as bass
import concourse.mybir as mybir
from concourse.bass_utils import run_bass_kernel_spmd
from concourse.aot_env import allow_ici_all_to_all

F32 = mybir.dt.float32
BF16 = mybir.dt.bfloat16
U16 = mybir.dt.uint16
AF = mybir.ActivationFunctionType
OP = mybir.AluOpType

B, L, DM, DI, DS, DC, FI = 4, 8192, 1024, 2048, 16, 4, 2816
NCORES = 8
RPC = B * L // NCORES
DPC = DI // NCORES
RH = RPC + 16
EPS = 1e-5

FC = [(i * 512, 512) for i in range(8)] + [(4096, 16)]
FC2 = [(i * 512, 512) for i in range(8)]
MAGIC = 16777216.0
GUARD = 1.0 - 2e-7


class Sem:
    def __init__(self, h):
        self.h = h
        self.n = 0


class Prog:
    ENGS = ("sync", "vector", "scalar", "tensor", "gpsimd")

    def __init__(self):
        self.items = []

    def op(self, eng, fn, waits=(), incs=()):
        assert eng in self.ENGS
        self.items.append((eng, fn, tuple(waits), tuple(incs)))
        for s, amt in incs:
            s.n += (16 if amt == 16 else (1 if amt == 1 else 0))

    def emit(self, block):
        def section(eng_name):
            def _(eng):
                for e, fn, waits, incs in self.items:
                    if e != eng_name:
                        continue
                    for s, v in waits:
                        if v > 0:
                            eng.wait_ge(s.h, v)
                    ins = fn(eng)
                    for s, amt in incs:
                        if amt is None:
                            ins.then_inc(s.h)
                        else:
                            ins.then_inc(s.h, amt)
            return _
        block.sync(section("sync"))
        block.vector(section("vector"))
        block.scalar(section("scalar"))
        block.tensor(section("tensor"))
        block.gpsimd(section("gpsimd"))


def build_program(dbg_names=()):
    nc = bass.Bass("TRN2", target_bir_lowering=False, debug=False)
    p = Prog()

    xin = nc.dram_tensor("x_rows", [RH, DM], F32, kind="ExternalInput").ap()
    w_in = nc.dram_tensor("w_in", [2 * DI, DM], F32, kind="ExternalInput").ap()
    conv_w = nc.dram_tensor("conv_w", [128, 16 * DC], F32, kind="ExternalInput").ap()
    conv_b = nc.dram_tensor("conv_b", [128, 16], F32, kind="ExternalInput").ap()
    w_xp = nc.dram_tensor("w_xp", [2 * DS + 1, DI], F32, kind="ExternalInput").ap()
    dt_w = nc.dram_tensor("dt_w", [128, 2], F32, kind="ExternalInput").ap()
    dt_b = nc.dram_tensor("dt_b", [128, 2], F32, kind="ExternalInput").ap()
    a_log = nc.dram_tensor("a_log", [128, 2 * DS], F32, kind="ExternalInput").ap()
    d_par = nc.dram_tensor("d_par", [128, 2], F32, kind="ExternalInput").ap()
    w_out = nc.dram_tensor("w_out", [DM, DI], F32, kind="ExternalInput").ap()
    w_g = nc.dram_tensor("w_g", [FI, DM], F32, kind="ExternalInput").ap()
    w_u = nc.dram_tensor("w_u", [FI, DM], F32, kind="ExternalInput").ap()
    w_d = nc.dram_tensor("w_d", [DM, FI], F32, kind="ExternalInput").ap()
    out = nc.dram_tensor("out", [RPC, DM], F32, kind="ExternalOutput").ap()

    d_xb = nc.dram_tensor("d_xb", [RH, DM], BF16).ap()
    d_winb = nc.dram_tensor("d_winb", [2 * DI, DM], BF16).ap()
    d_wxpb = nc.dram_tensor("d_wxpb", [48, DI], BF16).ap()
    d_woutb = nc.dram_tensor("d_woutb", [DM, DI], BF16).ap()
    d_wqgb = nc.dram_tensor("d_wqgb", [FI, DM], BF16).ap()
    d_wqub = nc.dram_tensor("d_wqub", [FI, DM], BF16).ap()
    d_wqdb = nc.dram_tensor("d_wqdb", [DM, FI], BF16).ap()
    d_ain = nc.dram_tensor("d_ain", [DI, RPC], BF16).ap()
    d_aout = nc.dram_tensor("d_aout", [DI, RPC], BF16).ap()
    d_z = nc.dram_tensor("d_z", [DI, RPC], BF16).ap()
    d_bcdo = nc.dram_tensor("d_bcdo", [2 * DS + 1, RPC], F32).ap()
    d_bcda = nc.dram_tensor("d_bcda", [NCORES * (2 * DS + 1), RPC], F32).ap()
    d_bcdb = nc.dram_tensor("d_bcdb", [NCORES * 2 * DS, RPC], BF16).ap()
    d_yin = nc.dram_tensor("d_yin", [DI, RPC], BF16).ap()
    d_yout = nc.dram_tensor("d_yout", [DI, RPC], BF16).ap()
    d_x1 = nc.dram_tensor("d_x1", [RPC, DM], BF16).ap()
    d_xq = nc.dram_tensor("d_xq", [RPC, DM], BF16).ap()
    d_hq = nc.dram_tensor("d_hq", [RPC, FI], BF16).ap()

    dbg = {}
    def dbg_tap(name, shape, dtype=F32):
        if name in dbg_names:
            dbg[name] = nc.dram_tensor("dbg_" + name, shape, dtype,
                                       kind="ExternalOutput").ap()
        return dbg.get(name)

    import contextlib
    stack = contextlib.ExitStack()
    nc._keepalive = stack
    phase_stack = [contextlib.ExitStack()]

    def sem(name):
        return Sem(stack.enter_context(nc.semaphore(name)))

    def sbuf(name, shape, dtype):
        return phase_stack[0].enter_context(nc.sbuf_tensor(name, shape, dtype))

    def psum(name, shape, dtype=F32):
        return phase_stack[0].enter_context(nc.psum_tensor(name, shape, dtype))

    def phase_end():
        phase_stack[0].close()
        phase_stack[0] = contextlib.ExitStack()

    s_dma = sem("s_dma")
    s_dmas = sem("s_dmas")
    s_pe = sem("s_pe")
    s_dve = sem("s_dve")
    s_act = sem("s_act")
    s_dmb = sem("s_dmb")
    s_cc = [sem(f"s_cc{i}") for i in range(3)]
    s_out = sem("s_out")
    s_hq = sem("s_hq")
    s_cx = sem("s_cx")
    s_wo = sem("s_wo")
    dly = stack.enter_context(nc.sbuf_tensor("dly", [128, 256], F32))

    def emit_quant(pp, Fq, x_f32, qout_bf, t, waits_in):
        nb = Fq // 16
        wq0 = list(waits_in) + [(s_dve, s_dve.n)]
        p.op("scalar", lambda e: e.activation(out=t["qg"][:pp, :Fq],
                                              in_=x_f32, func=AF.Sign),
             waits=wq0, incs=[(s_act, 1)])
        p.op("scalar", lambda e: e.activation(out=t["qa"][:pp, :Fq],
                                              in_=x_f32, func=AF.Abs),
             waits=wq0, incs=[(s_act, 1)])
        p.op("vector", lambda e: e.reduce_max(
            out=t["qm"][:pp, :nb].unsqueeze(2),
            in_=t["qa"][:pp, :Fq].rearrange("p (r n) -> p r n", n=16),
            axis=mybir.AxisListType.X),
            waits=waits_in + [(s_act, s_act.n)], incs=[(s_dve, 1)])
        p.op("vector", lambda e: e.tensor_scalar(
            out=t["qs"][:pp, :nb], in0=t["qm"][:pp, :nb],
            scalar1=1.0 / 6.0, scalar2=1e-12, op0=OP.mult, op1=OP.max),
            waits=[(s_dve, s_dve.n)], incs=[(s_dve, 1)])
        p.op("vector", lambda e: e.reciprocal(
            out=t["qi"][:pp, :nb], in_=t["qs"][:pp, :nb]),
            waits=[(s_dve, s_dve.n)], incs=[(s_dve, 1)])
        p.op("vector", lambda e: e.tensor_single_scalar(
            out=t["qsb"][:pp, :nb], in_=t["qs"][:pp, :nb], scalar=0.25,
            op=OP.mult), waits=[(s_dve, s_dve.n)], incs=[(s_dve, 1)])

        def bc(ap):
            return ap.unsqueeze(2).broadcast_to([pp, nb, 16])
        qa3 = t["qa"][:pp, :Fq].rearrange("p (r n) -> p r n", n=16)
        qb3 = t["qb"][:pp, :Fq].rearrange("p (r n) -> p r n", n=16)
        qg3 = t["qg"][:pp, :Fq].rearrange("p (r n) -> p r n", n=16)
        p.op("vector", lambda e: e.tensor_tensor(
            out=qb3, in0=qa3, in1=bc(t["qi"][:pp, :nb]), op=OP.mult),
            waits=[(s_dve, s_dve.n)], incs=[(s_dve, 1)])
        p.op("vector", lambda e: e.tensor_single_scalar(
            out=t["qc"][:pp, :Fq], in_=t["qb"][:pp, :Fq], scalar=2.0,
            op=OP.is_ge), waits=[(s_dve, s_dve.n)], incs=[(s_dve, 1)])
        p.op("vector", lambda e: e.tensor_single_scalar(
            out=t["qd"][:pp, :Fq], in_=t["qb"][:pp, :Fq], scalar=4.0,
            op=OP.is_ge), waits=[(s_dve, s_dve.n)], incs=[(s_dve, 1)])
        p.op("vector", lambda e: e.scalar_tensor_tensor(
            out=t["qe"][:pp, :Fq], in0=t["qc"][:pp, :Fq], scalar=-2.0,
            in1=t["qd"][:pp, :Fq], op0=OP.mult, op1=OP.subtract),
            waits=[(s_dve, s_dve.n)], incs=[(s_dve, 1)])
        p.op("vector", lambda e: e.tensor_single_scalar(
            out=t["qe"][:pp, :Fq], in_=t["qe"][:pp, :Fq], scalar=4.0,
            op=OP.add), waits=[(s_dve, s_dve.n)], incs=[(s_dve, 1)])
        p.op("vector", lambda e: e.tensor_tensor(
            out=t["qb"][:pp, :Fq], in0=t["qb"][:pp, :Fq],
            in1=t["qe"][:pp, :Fq], op=OP.mult),
            waits=[(s_dve, s_dve.n)], incs=[(s_dve, 1)])
        p.op("vector", lambda e: e.tensor_scalar(
            out=t["qb"][:pp, :Fq], in0=t["qb"][:pp, :Fq],
            scalar1=GUARD, scalar2=MAGIC, op0=OP.mult, op1=OP.add),
            waits=[(s_dve, s_dve.n)], incs=[(s_dve, 1)])
        p.op("vector", lambda e: e.tensor_single_scalar(
            out=t["qb"][:pp, :Fq], in_=t["qb"][:pp, :Fq], scalar=MAGIC,
            op=OP.subtract), waits=[(s_dve, s_dve.n)], incs=[(s_dve, 1)])
        p.op("vector", lambda e: e.scalar_tensor_tensor(
            out=t["qc"][:pp, :Fq], in0=t["qd"][:pp, :Fq], scalar=2.0,
            in1=t["qc"][:pp, :Fq], op0=OP.mult, op1=OP.add),
            waits=[(s_dve, s_dve.n)], incs=[(s_dve, 1)])
        p.op("vector", lambda e: e.tensor_single_scalar(
            out=t["qc"][:pp, :Fq], in_=t["qc"][:pp, :Fq], scalar=1.0,
            op=OP.add), waits=[(s_dve, s_dve.n)], incs=[(s_dve, 1)])
        p.op("vector", lambda e: e.tensor_tensor(
            out=t["qb"][:pp, :Fq], in0=t["qb"][:pp, :Fq],
            in1=t["qc"][:pp, :Fq], op=OP.mult),
            waits=[(s_dve, s_dve.n)], incs=[(s_dve, 1)])
        p.op("vector", lambda e: e.tensor_tensor(
            out=qa3, in0=qb3, in1=bc(t["qsb"][:pp, :nb]), op=OP.mult),
            waits=[(s_dve, s_dve.n)], incs=[(s_dve, 1)])
        p.op("vector", lambda e: e.tensor_tensor(
            out=qout_bf.rearrange("p (r n) -> p r n", n=16), in0=qa3,
            in1=qg3, op=OP.mult),
            waits=[(s_dve, s_dve.n), (s_act, s_act.n)], incs=[(s_dve, 1)])

    def emit_quant2_abs(pp, Fq, x_f32, t, waits_in):
        p.op("scalar", lambda e: e.activation(out=t["qa"][:pp, :Fq],
                                              in_=x_f32, func=AF.Abs),
             waits=list(waits_in), incs=[(s_act, 1)])
        return s_act.n

    def emit_quant2_sign(pp, Fq, x_f32, t, waits_in):
        p.op("scalar", lambda e: e.activation(out=t["qg"][:pp, :Fq],
                                              in_=x_f32, func=AF.Sign),
             waits=list(waits_in), incs=[(s_act, 1)])
        return s_act.n

    def emit_quant2_rest(pp, Fq, qout_bf, t, abs_mark, sign_mark, waits_in,
                         xn_out):
        nb = Fq // 16
        p.op("vector", lambda e: e.reduce_max(
            out=t["qm"][:pp, :nb].unsqueeze(2),
            in_=t["qa"][:pp, :Fq].rearrange("p (r n) -> p r n", n=16),
            axis=mybir.AxisListType.X),
            waits=list(waits_in) + [(s_act, abs_mark)], incs=[(s_dve, 1)])
        p.op("vector", lambda e: e.tensor_scalar(
            out=t["qs"][:pp, :nb], in0=t["qm"][:pp, :nb],
            scalar1=1.0 / 6.0, scalar2=1e-12, op0=OP.mult, op1=OP.max),
            waits=[(s_dve, s_dve.n)], incs=[(s_dve, 1)])
        p.op("vector", lambda e: e.reciprocal(
            out=t["qi"][:pp, :nb], in_=t["qs"][:pp, :nb]),
            waits=[(s_dve, s_dve.n)], incs=[(s_dve, 1)])

        def bc(ap):
            return ap.unsqueeze(2).broadcast_to([pp, nb, 16])
        qa3 = t["qa"][:pp, :Fq].rearrange("p (r n) -> p r n", n=16)
        qb3 = t["qb"][:pp, :Fq].rearrange("p (r n) -> p r n", n=16)
        qt = qout_bf
        p.op("vector", lambda e: e.tensor_tensor(
            out=qa3, in0=qa3, in1=bc(t["qi"][:pp, :nb]), op=OP.mult),
            waits=[(s_dve, s_dve.n)], incs=[(s_dve, 1)])
        p.op("vector", lambda e: e.tensor_scalar(
            out=t["qe"][:pp, :Fq], in0=t["qa"][:pp, :Fq], scalar1=2.0,
            scalar2=2.0, op0=OP.is_ge, op1=OP.mult),
            waits=[(s_dve, s_dve.n)], incs=[(s_dve, 1)])
        p.op("vector", lambda e: e.tensor_scalar(
            out=qt, in0=t["qa"][:pp, :Fq], scalar1=4.0, scalar2=None,
            op0=OP.is_ge),
            waits=[(s_dve, s_dve.n)], incs=[(s_dve, 1)])
        p.op("vector", lambda e: e.tensor_tensor(
            out=t["qe"][:pp, :Fq], in0=t["qe"][:pp, :Fq], in1=qt,
            op=OP.add), waits=[(s_dve, s_dve.n)], incs=[(s_dve, 1)])
        p.op("vector", lambda e: e.tensor_scalar(
            out=qt, in0=t["qe"][:pp, :Fq], scalar1=-1.0, scalar2=4.0,
            op0=OP.mult, op1=OP.add),
            waits=[(s_dve, s_dve.n)], incs=[(s_dve, 1)])
        p.op("vector", lambda e: e.tensor_tensor(
            out=t["qb"][:pp, :Fq], in0=t["qa"][:pp, :Fq], in1=qt,
            op=OP.mult),
            waits=[(s_dve, s_dve.n)], incs=[(s_dve, 1)])
        xn_out[0] = s_dve.n
        p.op("vector", lambda e: e.tensor_scalar(
            out=t["qb"][:pp, :Fq], in0=t["qb"][:pp, :Fq],
            scalar1=GUARD, scalar2=MAGIC, op0=OP.mult, op1=OP.add),
            waits=[(s_dve, s_dve.n)], incs=[(s_dve, 1)])
        p.op("vector", lambda e: e.tensor_single_scalar(
            out=t["qb"][:pp, :Fq], in_=t["qb"][:pp, :Fq], scalar=MAGIC,
            op=OP.subtract), waits=[(s_dve, s_dve.n)], incs=[(s_dve, 1)])
        p.op("vector", lambda e: e.tensor_scalar(
            out=qt, in0=t["qe"][:pp, :Fq], scalar1=1.0, scalar2=0.5,
            op0=OP.subtract, op1=OP.mult),
            waits=[(s_dve, s_dve.n)], incs=[(s_dve, 1)])
        p.op("vector", lambda e: e.tensor_tensor(
            out=t["qe"][:pp, :Fq], in0=t["qe"][:pp, :Fq], in1=qt,
            op=OP.mult), waits=[(s_dve, s_dve.n)], incs=[(s_dve, 1)])
        p.op("vector", lambda e: e.tensor_single_scalar(
            out=t["qe"][:pp, :Fq], in_=t["qe"][:pp, :Fq], scalar=1.0,
            op=OP.add), waits=[(s_dve, s_dve.n)], incs=[(s_dve, 1)])
        p.op("vector", lambda e: e.tensor_tensor(
            out=t["qb"][:pp, :Fq], in0=t["qb"][:pp, :Fq],
            in1=t["qe"][:pp, :Fq], op=OP.mult),
            waits=[(s_dve, s_dve.n)], incs=[(s_dve, 1)])
        p.op("vector", lambda e: e.tensor_tensor(
            out=qb3, in0=qb3, in1=bc(t["qs"][:pp, :nb]), op=OP.mult),
            waits=[(s_dve, s_dve.n)], incs=[(s_dve, 1)])
        p.op("vector", lambda e: e.scalar_tensor_tensor(
            out=qout_bf, in0=t["qb"][:pp, :Fq], scalar=0.25,
            in1=t["qg"][:pp, :Fq], op0=OP.mult, op1=OP.mult),
            waits=[(s_dve, s_dve.n), (s_act, sign_mark)], incs=[(s_dve, 1)])
        return s_dve.n

    def mk_qt2(tag, width):
        return {
            "qa": sbuf("qa" + tag, [128, width], F32).ap(),
            "qb": sbuf("qb" + tag, [128, width], F32).ap(),
            "qe": sbuf("qe" + tag, [128, width], BF16).ap(),
            "qg": sbuf("qg" + tag, [128, width], BF16).ap(),
            "qm": sbuf("qm" + tag, [128, width // 16], F32).ap(),
            "qs": sbuf("qs" + tag, [128, width // 16], F32).ap(),
            "qi": sbuf("qi" + tag, [128, width // 16], F32).ap(),
        }

    def mk_qt(tag, width):
        return {
            "qa": sbuf("qa" + tag, [128, width], F32).ap(),
            "qb": sbuf("qb" + tag, [128, width], F32).ap(),
            "qc": sbuf("qc" + tag, [128, width], BF16).ap(),
            "qd": sbuf("qd" + tag, [128, width], BF16).ap(),
            "qe": sbuf("qe" + tag, [128, width], BF16).ap(),
            "qm": sbuf("qm" + tag, [128, width // 16], F32).ap(),
            "qs": sbuf("qs" + tag, [128, width // 16], F32).ap(),
            "qi": sbuf("qi" + tag, [128, width // 16], F32).ap(),
            "qsb": sbuf("qsb" + tag, [128, width // 16], F32).ap(),
            "qg": sbuf("qg" + tag, [128, width], BF16).ap(),
        }

    p0a = sbuf("p0a", [128, 2 * 4096], F32)
    p0b = sbuf("p0b", [128, 2 * 4096], BF16)
    QT = mk_qt("0", 4096)

    cw_state = {"ld": [], "cv": [], "st": []}

    def conv_w32(src, dst, rows, cols, quant=False):
        if not quant:
            p.op("gpsimd", lambda e, src=src, dst=dst: e.dma_start(
                out=dst, in_=src), waits=[], incs=[(s_dmas, 16)])
            return
        ld, cv, st = cw_state["ld"], cw_state["cv"], cw_state["st"]
        ntile = (rows + 127) // 128
        for i in range(ntile):
            k = len(ld)
            par = k % 2
            r0 = i * 128
            pp = min(128, rows - r0)
            asl = p0a[:pp, par * 4096:par * 4096 + cols]
            bsl = p0b[:pp, par * 4096:par * 4096 + cols]
            w = []
            if k >= 2:
                w.append((s_dve, cv[k - 2]))
                w.append((s_dmas, st[k - 2]))
            p.op("sync", lambda e, r0=r0, pp=pp, src=src, cols=cols, asl=asl:
                 e.dma_start(out=asl, in_=src[r0:r0 + pp, :]),
                 waits=w, incs=[(s_dma, 16)])
            ld.append(s_dma.n)
            if quant:
                emit_quant(pp, cols, asl, bsl, QT, [(s_dma, ld[-1])])
            else:
                p.op("vector", lambda e, asl=asl, bsl=bsl: e.tensor_copy(
                    out=bsl, in_=asl),
                    waits=[(s_dma, ld[-1])], incs=[(s_dve, 1)])
            cv.append(s_dve.n)
            p.op("gpsimd", lambda e, r0=r0, pp=pp, dst=dst, bsl=bsl:
                 e.dma_start(out=dst[r0:r0 + pp, :], in_=bsl),
                 waits=[(s_dve, cv[-1])], incs=[(s_dmas, 16)])
            st.append(s_dmas.n)

    p.op("vector", lambda e: e.memset(dly[:, :], 0.0), waits=[],
         incs=[(s_dve, 1)])
    conv_w32(xin, d_xb, RH, DM)
    conv_w32(w_in, d_winb, 2 * DI, DM)
    conv_w32(w_xp, d_wxpb[0:2 * DS + 1, :], 2 * DS + 1, DI)
    conv_w32(w_out, d_woutb, DM, DI)
    conv_w32(w_g, d_wqgb, FI, DM, quant=True)
    conv_w32(w_u, d_wqub, FI, DM, quant=True)
    conv_w32(w_d, d_wqdb, DM, FI, quant=True)
    ph0 = [(s_dmas, s_dmas.n), (s_dma, s_dma.n), (s_dve, s_dve.n),
           (s_act, s_act.n)]
    phase_end()

    xT = sbuf("xT", [128, 8 * RH], BF16)
    winT = sbuf("winT", [128, 8 * 4096], BF16)
    wxpT = sbuf("wxpT", [128, 16 * 48], BF16)
    xpre = sbuf("xpre", [128, RH], BF16)
    cvA = sbuf("cvA", [128, 4096], F32)
    cvB = sbuf("cvB", [128, 4096], BF16)
    xt_t = sbuf("xt_t", [128, 2 * 4096], BF16)
    zbuf = sbuf("zbuf", [128, RH], BF16)
    bcda = sbuf("bcda", [33, 4096], F32)
    cwt = sbuf("cwt", [128, 16 * DC], F32)
    cbt = sbuf("cbt", [128, 16], F32)
    p_in = [psum("p_in0", [128, 512]), psum("p_in1", [128, 512])]
    p_bcd = [psum("p_bcd0", [33, 512]), psum("p_bcd1", [33, 512])]

    for kc in range(8):
        p.op("sync", lambda e, kc=kc: e.dma_start(
            out=xT[:, kc * RH:(kc + 1) * RH],
            in_=d_xb[:, kc * 128:(kc + 1) * 128], transpose=True),
            waits=ph0, incs=[(s_dma, 16)])
        p.op("sync", lambda e, kc=kc: e.dma_start(
            out=winT[:, kc * 4096:(kc + 1) * 4096],
            in_=d_winb[:, kc * 128:(kc + 1) * 128], transpose=True),
            waits=ph0, incs=[(s_dma, 16)])
    for kc in range(16):
        p.op("sync", lambda e, kc=kc: e.dma_start(
            out=wxpT[:, kc * 48:(kc + 1) * 48],
            in_=d_wxpb[:, kc * 128:(kc + 1) * 128], transpose=True),
            waits=ph0, incs=[(s_dma, 16)])
    p.op("sync", lambda e: e.dma_start(out=cwt[:, :], in_=conv_w),
         waits=ph0, incs=[(s_dmb, 16)])
    p.op("sync", lambda e: e.dma_start(out=cbt[:, :], in_=conv_b),
         waits=ph0, incs=[(s_dmb, 16)])
    cw_ld = s_dmb.n
    p.op("scalar", lambda e: e.activation(
        out=dly[:, :], in_=dly[:, :], func=AF.Square),
        waits=[(s_dma, s_dma.n)], incs=[(s_act, 1)])
    lds1 = s_dma.n
    lds1_gate = s_act.n

    evac_marks = []
    silu_marks = {}
    xp_marks = {}
    bcd_marks = []
    nchunk = 0

    def emit_xproj(mi):
        par = mi % 2
        for fi2, (g0, gw) in enumerate(FC2):
            bank = fi2 % 2

            def xpfn(e, mi=mi, g0=g0, gw=gw, bank=bank, par=par):
                return e.matmul(
                    p_bcd[bank][:, :gw],
                    wxpT[:, mi * 48:mi * 48 + 33],
                    xt_t[:, par * 4096 + g0:par * 4096 + g0 + gw],
                    start=True, stop=True)
            w = [(s_act, silu_marks[mi])]
            if len(bcd_marks) >= 2:
                w.append((s_dve, bcd_marks[-2]))
            p.op("tensor", xpfn, waits=w, incs=[(s_pe, 1)])
            pe_now = s_pe.n
            if mi == 0:
                p.op("vector", lambda e, g0=g0, gw=gw, bank=bank:
                     e.tensor_copy(out=bcda[:, g0:g0 + gw],
                                   in_=p_bcd[bank][:, :gw]),
                     waits=[(s_pe, pe_now)], incs=[(s_dve, 1)])
            else:
                p.op("vector", lambda e, g0=g0, gw=gw, bank=bank:
                     e.tensor_tensor(out=bcda[:, g0:g0 + gw],
                                     in0=bcda[:, g0:g0 + gw],
                                     in1=p_bcd[bank][:, :gw], op=OP.add),
                     waits=[(s_pe, pe_now)], incs=[(s_dve, 1)])
            bcd_marks.append(s_dve.n)
        xp_marks[mi] = s_pe.n

    for mi in range(32):
        is_x = mi < 16
        for fi, (f0, fw) in enumerate(FC):
            bank = nchunk % 2
            w = [(s_dma, lds1), (s_act, lds1_gate)]
            if nchunk >= 2:
                w.append((s_dve, evac_marks[nchunk - 2]))

            def mmfn(e, mi=mi, f0=f0, fw=fw, bank=bank):
                ins = None
                for kc in range(8):
                    ins = e.matmul(
                        p_in[bank][:, :fw],
                        winT[:, kc * 4096 + mi * 128:kc * 4096 + (mi + 1) * 128],
                        xT[:, kc * RH + f0:kc * RH + f0 + fw],
                        start=(kc == 0), stop=(kc == 7))
                return ins
            p.op("tensor", mmfn, waits=w, incs=[(s_pe, 1)])
            nchunk += 1
            pe_now = s_pe.n
            if is_x:
                p.op("vector", lambda e, f0=f0, fw=fw, bank=bank: e.tensor_copy(
                    out=xpre[:, f0:f0 + fw], in_=p_in[bank][:, :fw]),
                    waits=[(s_pe, pe_now)], incs=[(s_dve, 1)])
            else:
                p.op("vector", lambda e, f0=f0, fw=fw, bank=bank:
                     e.tensor_copy(
                         out=zbuf[:, f0:f0 + fw],
                         in_=p_in[bank][:, :fw]),
                     waits=[(s_pe, pe_now), (s_dmas, s_dmas.n)],
                     incs=[(s_dve, 1)])
            evac_marks.append(s_dve.n)

        if is_x:
            par = mi % 2
            p.op("vector", lambda e, mi=mi: e.tensor_scalar(
                out=cvA[:, :], in0=xpre[:, 0:4096],
                scalar1=cwt[:, mi * 4:mi * 4 + 1], scalar2=None,
                op0=OP.mult),
                waits=[(s_dve, s_dve.n), (s_dmb, cw_ld)], incs=[(s_dve, 1)])
            cvp = [cvA, cvB, cvA, cvB]
            for j in range(1, 4):
                p.op("vector", lambda e, j=j, mi=mi, src=cvp[j - 1],
                     dst=cvp[j]: e.scalar_tensor_tensor(
                         out=dst[:, :], in0=xpre[:, j:j + 4096],
                         scalar=cwt[:, mi * 4 + j:mi * 4 + j + 1],
                         in1=src[:, :], op0=OP.mult, op1=OP.add),
                     waits=[(s_dve, s_dve.n)], incs=[(s_dve, 1)])
            w = [(s_dve, s_dve.n), (s_dmb, cw_ld)]
            if mi >= 2:
                w.append((s_pe, xp_marks[mi - 2]))
                w.append((s_dmas, s_dmas.n))
            p.op("scalar", lambda e, mi=mi, par=par: e.activation(
                out=xt_t[:, par * 4096:(par + 1) * 4096], in_=cvB[:, :],
                func=AF.Silu, bias=cbt[:, mi:mi + 1]),
                waits=w, incs=[(s_act, 1)])
            silu_marks[mi] = s_act.n
            p.op("gpsimd", lambda e, mi=mi, par=par: e.dma_start(
                out=d_ain[mi * 128:(mi + 1) * 128, :],
                in_=xt_t[:, par * 4096:(par + 1) * 4096]),
                waits=[(s_act, silu_marks[mi])], incs=[(s_dmas, 16)])
            if mi >= 1:
                emit_xproj(mi - 1)
        else:
            if mi == 16:
                emit_xproj(15)
            p.op("gpsimd", lambda e, mi=mi: e.dma_start(
                out=d_z[(mi - 16) * 128:(mi - 15) * 128, :],
                in_=zbuf[:, 3:3 + RPC]),
                waits=[(s_dve, evac_marks[-1])], incs=[(s_dmas, 16)])

    if dbg_tap("wqg", [FI, DM], BF16) is not None:
        p.op("gpsimd", lambda e: e.dma_start(out=dbg["wqg"], in_=d_wqgb),
             waits=[(s_dmas, s_dmas.n)], incs=[(s_out, 16)])
    if dbg_tap("bcd", [33, 4096]) is not None:
        p.op("gpsimd", lambda e: e.dma_start(out=dbg["bcd"], in_=bcda[:, :]),
             waits=[(s_dve, s_dve.n)], incs=[(s_out, 16)])
    if dbg_tap("xtilde", [DI, RPC], BF16) is not None:
        p.op("gpsimd", lambda e: e.dma_start(out=dbg["xtilde"], in_=d_ain),
             waits=[(s_dmas, s_dmas.n)], incs=[(s_out, 16)])
    if dbg_tap("ztap", [DI, RPC], BF16) is not None:
        p.op("gpsimd", lambda e: e.dma_start(out=dbg["ztap"], in_=d_z),
             waits=[(s_dmas, s_dmas.n)], incs=[(s_out, 16)])

    p.op("gpsimd", lambda e: e.dma_start(out=d_bcdo, in_=bcda[:, :]),
         waits=[(s_dve, s_dve.n)], incs=[(s_dmas, 16)])
    p.op("gpsimd", lambda e: e.collective_compute(
        "AllGather", OP.bypass, replica_groups=[list(range(NCORES))],
        ins=[d_bcdo.opt()], outs=[d_bcda.opt()]),
        waits=[(s_dmas, s_dmas.n)], incs=[(s_cc[0], None)])
    def _a2a(e, src, dst):
        with allow_ici_all_to_all():
            return e.collective_compute(
                "AllToAll", OP.bypass, replica_groups=[list(range(NCORES))],
                ins=[src.opt()], outs=[dst.opt()])
    p.op("gpsimd", lambda e: _a2a(e, d_ain, d_aout),
         waits=[(s_dmas, s_dmas.n)], incs=[(s_cc[1], None)])
    p.op("gpsimd", lambda e: e.dma_start(
        out=d_bcdb.rearrange("(r e) f -> r e f", e=32),
        in_=d_bcda.rearrange("(r e) f -> r e f", e=33)[:, 0:32, :]),
        waits=[(s_cc[0], 1)], incs=[(s_dmas, 16)])

    ph1 = [(s_dmas, s_dmas.n), (s_dma, s_dma.n), (s_dve, s_dve.n),
           (s_act, s_act.n), (s_pe, s_pe.n), (s_cc[0], 1), (s_cc[1], 1)]
    phase_end()

    LC = 4096
    xt2 = sbuf("xt2", [128, 2 * LC], BF16)
    dt2 = sbuf("dt2", [128, 2 * LC], BF16)
    u2 = sbuf("u2", [128, 2 * LC], BF16)
    yac = sbuf("yac", [128, 2 * LC], BF16)
    dAa = sbuf("dAa", [128, 2 * LC], BF16)
    scnb = sbuf("scnb", [128, 2 * LC], BF16)
    dBx = scnb[:, 0:LC]
    hh = scnb[:, LC:2 * LC]
    etmp = sbuf("etmp", [128, 2 * LC], BF16)
    brep = sbuf("brep", [128, 2 * LC], BF16)
    crep = sbuf("crep", [128, 2 * LC], BF16)
    dtr = sbuf("dtr", [128, LC], F32)
    tm1 = etmp.ap().bitcast(F32)
    tm2 = scnb.ap().bitcast(F32)
    asc = sbuf("asc", [128, 2 * DS], F32)
    dtw2 = sbuf("dtw2", [128, 2], F32)
    dtb2 = sbuf("dtb2", [128, 2], F32)
    dpar2 = sbuf("dpar2", [128, 2], F32)
    st2 = sbuf("st2", [128, 32], F32)

    p.op("sync", lambda e: e.dma_start(out=dtw2[:, :], in_=dt_w),
         waits=ph1, incs=[(s_dma, 16)])
    p.op("sync", lambda e: e.dma_start(out=dtb2[:, :], in_=dt_b),
         waits=[], incs=[(s_dma, 16)])
    p.op("sync", lambda e: e.dma_start(out=dpar2[:, :], in_=d_par),
         waits=[], incs=[(s_dma, 16)])
    p.op("sync", lambda e: e.dma_start(out=asc[:, :], in_=a_log),
         waits=[], incs=[(s_dma, 16)])
    p.op("scalar", lambda e: e.activation(out=asc[:, :], in_=asc[:, :],
                                          func=AF.Exp),
         waits=[(s_dma, s_dma.n)], incs=[(s_act, 1)])
    p.op("vector", lambda e: e.tensor_single_scalar(
        out=asc[:, :], in_=asc[:, :], scalar=-1.0, op=OP.mult),
        waits=[(s_act, s_act.n)], incs=[(s_dve, 1)])
    asc_done = s_dve.n

    scan_marks = []
    dbx_marks = []
    emul_marks = []
    pool_marks = []
    git = 0
    for b in range(B):
        for lc in range(2):
            blk = 2 * b + lc
            w_head = [(s_dve, s_dve.n if blk == 0 else pre_n),
                      (s_act, 0 if blk == 0 else asc_done and 0)]
            w_head = [x for x in w_head if x[1] > 0]
            if blk == 0:
                w_head = [(s_dve, asc_done), (s_dmas, s_dmas.n)]
            p.op("sync", lambda e, blk=blk: e.dma_start(
                out=xt2[:, 0:LC], in_=d_aout[blk * 256:blk * 256 + 128, :]),
                waits=w_head, incs=[(s_dma, 16)])
            p.op("sync", lambda e, blk=blk: e.dma_start(
                out=xt2[:, LC:2 * LC],
                in_=d_aout[blk * 256 + 128:blk * 256 + 256, :]),
                waits=[], incs=[(s_dma, 16)])
            p.op("sync", lambda e, blk=blk: e.dma_start(
                out=dtr[:, :],
                in_=d_bcda[blk * 33 + 32:blk * 33 + 33, :]
                .partition_broadcast(128)),
                waits=[], incs=[(s_dma, 16)])
            ld2 = s_dma.n
            for dtile in range(2):
                sl = slice(dtile * LC, (dtile + 1) * LC)
                p.op("vector", lambda e, dtile=dtile: e.tensor_scalar(
                    out=tm1, in0=dtr[:, :],
                    scalar1=dtw2[:, dtile:dtile + 1],
                    scalar2=dtb2[:, dtile:dtile + 1],
                    op0=OP.mult, op1=OP.add),
                    waits=[(s_dma, ld2), (s_act, s_act.n)],
                    incs=[(s_dve, 1)])
                p.op("scalar", lambda e: e.activation(
                    out=tm2, in_=tm1, func=AF.Abs),
                    waits=[(s_dve, s_dve.n)], incs=[(s_act, 1)])
                p.op("scalar", lambda e: e.activation(
                    out=tm2, in_=tm2, func=AF.Exp, scale=-1.0),
                    waits=[(s_act, s_act.n)], incs=[(s_act, 1)])
                p.op("scalar", lambda e: e.activation(
                    out=tm2, in_=tm2, func=AF.Ln, bias=1.0),
                    waits=[(s_act, s_act.n)], incs=[(s_act, 1)])
                p.op("vector", lambda e: e.tensor_single_scalar(
                    out=tm1, in_=tm1, scalar=0.0, op=OP.max),
                    waits=[(s_dve, s_dve.n)], incs=[(s_dve, 1)])
                p.op("vector", lambda e, sl=sl: e.tensor_tensor(
                    out=dt2[:, sl], in0=tm1, in1=tm2, op=OP.add),
                    waits=[(s_dve, s_dve.n), (s_act, s_act.n)],
                    incs=[(s_dve, 1)])
                p.op("vector", lambda e, sl=sl: e.tensor_tensor(
                    out=u2[:, sl], in0=dt2[:, sl], in1=xt2[:, sl],
                    op=OP.mult),
                    waits=[(s_dve, s_dve.n)], incs=[(s_dve, 1)])
                p.op("vector", lambda e, sl=sl, dtile=dtile: e.tensor_scalar(
                    out=yac[:, sl], in0=xt2[:, sl],
                    scalar1=dpar2[:, dtile:dtile + 1], scalar2=None,
                    op0=OP.mult),
                    waits=[(s_dve, s_dve.n), (s_dmas, s_dmas.n)],
                    incs=[(s_dve, 1)])
            pre_n = s_dve.n

            for n in range(DS):
                npar = n % 2
                bsl = slice(npar * LC, (npar + 1) * LC)
                wb = []
                if len(dbx_marks) >= 3:
                    wb.append((s_dve, dbx_marks[-3]))
                p.op("sync", lambda e, blk=blk, n=n, bsl=bsl: e.dma_start(
                    out=brep[:, bsl],
                    in_=d_bcdb[blk * 32 + n:blk * 32 + n + 1, :]
                    .partition_broadcast(128)),
                    waits=wb, incs=[(s_dma, 16)])
                wc = []
                if len(emul_marks) >= 3:
                    wc.append((s_dve, emul_marks[-3]))
                p.op("scalar", lambda e, blk=blk, n=n, bsl=bsl: e.dma_start(
                    out=crep[:, bsl],
                    in_=d_bcdb[blk * 32 + 16 + n:blk * 32 + 17 + n, :]
                    .partition_broadcast(128)),
                    waits=wc, incs=[(s_dmb, 16)])
                bc_ld = s_dma.n
                bc_ldb = s_dmb.n
                for dtile in range(2):
                    sl = slice(dtile * LC, (dtile + 1) * LC)
                    apar = git % 2
                    asl = slice(apar * LC, (apar + 1) * LC)
                    aw = [(s_dve, pre_n)]
                    if git >= 2:
                        aw.append((s_dve, scan_marks[git - 2]))
                    p.op("scalar", lambda e, sl=sl, asl=asl, dtile=dtile, n=n:
                         e.activation(
                             out=dAa[:, asl], in_=dt2[:, sl], func=AF.Exp,
                             scale=asc[:, dtile * DS + n:dtile * DS + n + 1]),
                         waits=aw, incs=[(s_act, 1)])
                    act_n = s_act.n
                    p.op("vector", lambda e, sl=sl, bsl=bsl: e.tensor_tensor(
                        out=dBx, in0=u2[:, sl], in1=brep[:, bsl],
                        op=OP.mult),
                        waits=[(s_dma, bc_ld), (s_dmb, bc_ldb),
                               (s_dve, s_dve.n)],
                        incs=[(s_dve, 1)])
                    dbx_marks.append(s_dve.n)
                    sidx = 2 * n + dtile
                    if lc == 0:
                        init = 0.0
                    else:
                        init = st2[:, sidx:sidx + 1]
                    p.op("vector", lambda e, asl=asl, init=init:
                         e.tensor_tensor_scan(
                             out=hh,
                             data0=dAa[:, asl], data1=dBx,
                             initial=init, op0=OP.mult, op1=OP.add),
                         waits=[(s_act, act_n), (s_dve, s_dve.n)],
                         incs=[(s_dve, 1)])
                    scan_marks.append(s_dve.n)
                    if lc == 0:
                        p.op("vector", lambda e, sidx=sidx: e.tensor_copy(
                            out=st2[:, sidx:sidx + 1], in_=hh[:, LC - 1:LC]),
                            waits=[(s_dve, s_dve.n)], incs=[(s_dve, 1)])
                    ew = [(s_dve, s_dve.n)]
                    p.op("vector", lambda e, bsl=bsl, epar=git % 2:
                         e.tensor_tensor(
                             out=etmp[:, epar * LC:(epar + 1) * LC],
                             in0=hh, in1=crep[:, bsl], op=OP.mult),
                         waits=ew, incs=[(s_dve, 1)])
                    emul_marks.append(s_dve.n)
                    p.op("vector", lambda e, sl=sl, epar=git % 2:
                         e.tensor_tensor(
                             out=yac[:, sl], in0=yac[:, sl],
                             in1=etmp[:, epar * LC:(epar + 1) * LC],
                             op=OP.add),
                         waits=[(s_dve, emul_marks[-1])],
                         incs=[(s_dve, 1)])
                    pool_marks.append(s_dve.n)
                    git += 1
            for dtile in range(2):
                sl = slice(dtile * LC, (dtile + 1) * LC)
                p.op("gpsimd", lambda e, blk=blk, dtile=dtile, sl=sl:
                     e.dma_start(
                         out=d_yin[blk * 256 + dtile * 128:
                                   blk * 256 + (dtile + 1) * 128, :],
                         in_=yac[:, sl]),
                     waits=[(s_dve, s_dve.n)], incs=[(s_dmas, 16)])

    p.op("gpsimd", lambda e: _a2a(e, d_yin, d_yout),
         waits=[(s_dmas, s_dmas.n)], incs=[(s_cc[2], None)])

    if dbg_tap("ytap", [DI, RPC], BF16) is not None:
        p.op("gpsimd", lambda e: e.dma_start(out=dbg["ytap"], in_=d_yin),
             waits=[(s_dmas, s_dmas.n)], incs=[(s_out, 16)])

    ph2 = [(s_dmas, s_dmas.n), (s_dma, s_dma.n), (s_dve, s_dve.n),
           (s_act, s_act.n), (s_cc[2], 1)]
    phase_end()

    gbuf = sbuf("gbuf", [128, 16 * 2048], BF16)
    woT = sbuf("woT", [128, 16 * DM], BF16)
    ytl = sbuf("ytl", [128, 2 * 2048], BF16)
    ztl = sbuf("ztl", [128, 2 * 2048], BF16)
    szl = sbuf("szl", [128, 2 * 2048], BF16)
    a1t = sbuf("a1t", [128, 2 * DM], F32)
    a2t = sbuf("a2t", [128, 2 * DM], F32)
    xrow = sbuf("xrow", [128, 2 * DM], BF16)
    x1t = sbuf("x1t", [128, 2 * DM], BF16)
    x1f = sbuf("x1f", [128, 2 * DM], F32)
    lnt = sbuf("lnt", [128, 16], F32)
    qo = sbuf("qo", [128, DM], BF16)
    epsA = sbuf("epsA", [128, 1], F32)
    QTA = mk_qt("a", DM)
    p_op = [[psum("p_op00", [128, 512]), psum("p_op01", [128, 512])],
            [psum("p_op10", [128, 512]), psum("p_op11", [128, 512])]]

    for kc in range(16):
        p.op("sync", lambda e, kc=kc: e.dma_start(
            out=woT[:, kc * DM:(kc + 1) * DM],
            in_=d_woutb[:, kc * 128:(kc + 1) * 128], transpose=True),
            waits=ph2, incs=[(s_wo, 16)])
    wo_ld = s_wo.n
    p.op("scalar", lambda e: e.activation(
        out=dly[:, :], in_=dly[:, :], func=AF.Square),
        waits=[(s_wo, wo_ld)], incs=[(s_act, 1)])
    wo_gate = s_act.n
    p.op("vector", lambda e: e.memset(epsA[:, :], EPS), waits=ph2,
         incs=[(s_dve, 1)])

    res_marks = {}
    xrow_marks = {}
    qo_marks = {}
    x1dma_marks = {}
    g_done_at = {}

    def a_mm(tt):
        half, ttl, par = tt // 16, tt % 16, tt % 2
        for fh in range(2):
            def opfn(e, ttl=ttl, fh=fh, par=par, half=half):
                ins = None
                for kc in range(16):
                    ins = e.matmul(
                        p_op[par][fh][:, :],
                        gbuf[:, kc * 2048 + ttl * 128:
                             kc * 2048 + (ttl + 1) * 128],
                        woT[:, kc * DM + fh * 512:kc * DM + (fh + 1) * 512],
                        start=(kc == 0), stop=(kc == 15))
                return ins
            w = [(s_dve, g_done_at[half]), (s_wo, wo_ld),
                 (s_act, wo_gate)]
            if tt >= 2:
                w.append((s_dve, res_marks[tt - 2]))
            p.op("tensor", opfn, waits=w, incs=[(s_pe, 1)])
        res_marks[(tt, "pe")] = s_pe.n
        w = [(s_dve, xrow_marks.get(tt - 2, 0))]
        psl = slice(par * DM, (par + 1) * DM)
        p.op("sync", lambda e, tt=tt, psl=psl: e.dma_start(
            out=xrow[:, psl], in_=d_xb[3 + tt * 128:3 + (tt + 1) * 128, :]),
            waits=[x for x in w if x[1] > 0], incs=[(s_dma, 16)])
        res_marks[(tt, "xd")] = s_dma.n

    def a_s1(tt):
        par = tt % 2
        psl = slice(par * DM, (par + 1) * DM)
        p.op("vector", lambda e, psl=psl: e.tensor_copy(
            out=a2t[:, psl], in_=xrow[:, psl]),
            waits=[(s_dma, res_marks[(tt, "xd")]), (s_act, s_act.n)],
            incs=[(s_dve, 1)])
        xrow_marks[tt] = s_dve.n
        p.op("vector", lambda e, par=par: e.tensor_tensor(
            out=a1t[:, par * DM:par * DM + 512],
            in0=a2t[:, par * DM:par * DM + 512],
            in1=p_op[par][0][:, :], op=OP.add),
            waits=[(s_pe, res_marks[(tt, "pe")]), (s_dve, s_dve.n)],
            incs=[(s_dve, 1)])
        p.op("vector", lambda e, par=par: e.tensor_tensor(
            out=a1t[:, par * DM + 512:par * DM + 1024],
            in0=a2t[:, par * DM + 512:par * DM + 1024],
            in1=p_op[par][1][:, :], op=OP.add),
            waits=[(s_dve, s_dve.n)], incs=[(s_dve, 1)])
        res_marks[tt] = s_dve.n
        p.op("vector", lambda e, par=par, psl=psl: e.reduce_sum(
            out=lnt[:, par * 8:par * 8 + 1], in_=a1t[:, psl],
            axis=mybir.AxisListType.X),
            waits=[(s_dve, s_dve.n)], incs=[(s_dve, 1)])
        p.op("vector", lambda e, par=par: e.tensor_single_scalar(
            out=lnt[:, par * 8:par * 8 + 1],
            in_=lnt[:, par * 8:par * 8 + 1], scalar=1.0 / DM,
            op=OP.mult), waits=[(s_dve, s_dve.n)], incs=[(s_dve, 1)])
        p.op("scalar", lambda e, psl=psl: e.activation(
            out=a2t[:, psl], in_=a1t[:, psl], func=AF.Square),
            waits=[(s_dve, s_dve.n)], incs=[(s_act, 1)])
        res_marks[(tt, "sq")] = s_act.n

    def a_s2(tt):
        par = tt % 2
        psl = slice(par * DM, (par + 1) * DM)
        p.op("vector", lambda e, par=par, psl=psl: e.reduce_sum(
            out=lnt[:, par * 8 + 1:par * 8 + 2], in_=a2t[:, psl],
            axis=mybir.AxisListType.X),
            waits=[(s_act, res_marks[(tt, "sq")])], incs=[(s_dve, 1)])
        p.op("vector", lambda e, par=par: e.tensor_scalar(
            out=lnt[:, par * 8 + 2:par * 8 + 3],
            in0=lnt[:, par * 8:par * 8 + 1],
            scalar1=lnt[:, par * 8:par * 8 + 1], scalar2=None,
            op0=OP.mult),
            waits=[(s_dve, s_dve.n)], incs=[(s_dve, 1)])
        p.op("vector", lambda e, par=par: e.tensor_scalar(
            out=lnt[:, par * 8 + 3:par * 8 + 4],
            in0=lnt[:, par * 8 + 1:par * 8 + 2], scalar1=1.0 / DM,
            scalar2=lnt[:, par * 8 + 2:par * 8 + 3], op0=OP.mult,
            op1=OP.subtract),
            waits=[(s_dve, s_dve.n)], incs=[(s_dve, 1)])
        p.op("scalar", lambda e, par=par: e.activation(
            out=lnt[:, par * 8 + 4:par * 8 + 5],
            in_=lnt[:, par * 8 + 3:par * 8 + 4], func=AF.Ln,
            bias=epsA[:, 0:1]),
            waits=[(s_dve, s_dve.n)], incs=[(s_act, 1)])
        p.op("scalar", lambda e, par=par: e.activation(
            out=lnt[:, par * 8 + 5:par * 8 + 6],
            in_=lnt[:, par * 8 + 4:par * 8 + 5], func=AF.Exp, scale=-0.5),
            waits=[(s_act, s_act.n)], incs=[(s_act, 1)])
        res_marks[(tt, "rstd")] = s_act.n

    def a_s3(tt):
        par = tt % 2
        psl = slice(par * DM, (par + 1) * DM)
        p.op("vector", lambda e, par=par, psl=psl: e.tensor_scalar(
            out=x1f[:, psl], in0=a1t[:, psl],
            scalar1=lnt[:, par * 8:par * 8 + 1],
            scalar2=lnt[:, par * 8 + 5:par * 8 + 6],
            op0=OP.subtract, op1=OP.mult),
            waits=[(s_act, res_marks[(tt, "rstd")]), (s_dve, s_dve.n),
                   (s_dmas, s_dmas.n)],
            incs=[(s_dve, 1)])
        wx1 = [(s_dve, s_dve.n)]
        if tt >= 2:
            wx1.append((s_dmas, x1dma_marks[tt - 2]))
        p.op("vector", lambda e, psl=psl: e.tensor_copy(
            out=x1t[:, psl], in_=x1f[:, psl]),
            waits=wx1, incs=[(s_dve, 1)])
        res_marks[(tt, "x1")] = s_dve.n
        p.op("gpsimd", lambda e, tt=tt, psl=psl: e.dma_start(
            out=d_x1[tt * 128:(tt + 1) * 128, :], in_=x1t[:, psl]),
            waits=[(s_dve, res_marks[(tt, "x1")])], incs=[(s_dmas, 16)])
        x1dma_marks[tt] = s_dmas.n

    def a_q(tt):
        par = tt % 2
        psl = slice(par * DM, (par + 1) * DM)
        wq = [(s_dve, res_marks[(tt, "x1")])]
        if tt >= 1:
            wq.append((s_dmas, qo_marks[tt - 1]))
        emit_quant(128, DM, x1f[:, psl], qo[:, :], QTA, wq)
        p.op("gpsimd", lambda e, tt=tt: e.dma_start(
            out=d_xq[tt * 128:(tt + 1) * 128, :], in_=qo[:, :]),
            waits=[(s_dve, s_dve.n)], incs=[(s_dmas, 16)])
        qo_marks[tt] = s_dmas.n

    gtt = {}
    silu_g = {}
    yld = {}
    pe_half = {}
    for half in range(2):
        t0 = half * 2048
        for kc in range(16):
            gkc = half * 16 + kc
            q = gkc % 2
            qsl = slice(q * 2048, (q + 1) * 2048)
            wy = [(s_wo, wo_ld)]
            if gkc >= 2:
                wy.append((s_dve, gtt[gkc - 2]))
                wy.append((s_act, silu_g[gkc - 2]))
            p.op("sync", lambda e, kc=kc, t0=t0, qsl=qsl: e.dma_start(
                out=ytl[:, qsl],
                in_=d_yout[kc * 128:(kc + 1) * 128, t0:t0 + 2048]),
                waits=wy, incs=[(s_dma, 16)])
            p.op("sync", lambda e, kc=kc, t0=t0, qsl=qsl: e.dma_start(
                out=ztl[:, qsl],
                in_=d_z[kc * 128:(kc + 1) * 128, t0:t0 + 2048]),
                waits=[], incs=[(s_dma, 16)])
            yld[gkc] = s_dma.n
            ws = [(s_dma, yld[gkc])]
            if gkc >= 2:
                ws.append((s_dve, gtt[gkc - 2]))
            p.op("scalar", lambda e, qsl=qsl: e.activation(
                out=szl[:, qsl], in_=ztl[:, qsl], func=AF.Silu),
                waits=ws, incs=[(s_act, 1)])
            silu_g[gkc] = s_act.n
            wt = [(s_act, silu_g[gkc]), (s_dma, yld[gkc])]
            if half == 1:
                wt.append((s_pe, pe_half[0]))
            p.op("vector", lambda e, kc=kc, qsl=qsl: e.tensor_tensor(
                out=gbuf[:, kc * 2048:(kc + 1) * 2048], in0=ytl[:, qsl],
                in1=szl[:, qsl], op=OP.mult),
                waits=wt, incs=[(s_dve, 1)])
            gtt[gkc] = s_dve.n
        g_done_at[half] = s_dve.n
        base = half * 16
        for i in range(16):
            tt = base + i
            a_mm(tt)
            a_s1(tt)
            if i >= 1:
                a_s3(tt - 1)
                a_q(tt - 1)
            a_s2(tt)
        a_s3(base + 15)
        a_q(base + 15)
        pe_half[half] = s_pe.n

    if dbg_tap("x1tap", [RPC, DM], BF16) is not None:
        p.op("gpsimd", lambda e: e.dma_start(out=dbg["x1tap"], in_=d_x1),
             waits=[(s_dmas, s_dmas.n)], incs=[(s_out, 16)])
    if dbg_tap("xqtap", [RPC, DM], BF16) is not None:
        p.op("gpsimd", lambda e: e.dma_start(out=dbg["xqtap"], in_=d_xq),
             waits=[(s_dmas, s_dmas.n)], incs=[(s_out, 16)])
    pa = [(s_dmas, s_dmas.n), (s_dma, s_dma.n), (s_dve, s_dve.n),
          (s_act, s_act.n), (s_pe, s_pe.n)]
    phase_end()

    xqT = sbuf("xqT", [128, 8 * RPC], BF16)
    wgT = sbuf("wgT", [128, 8 * FI], BF16)
    wuT = sbuf("wuT", [128, 8 * FI], BF16)
    sgf = sbuf("sgf", [128, 512], F32)
    htf = sbuf("htf", [128, 2 * FI], BF16)
    htq = sbuf("htq", [128, FI], BF16)
    QTB = mk_qt2("b", FI)
    p_g = [psum("p_g0", [128, 512]), psum("p_g1", [128, 512])]
    p_u = [psum("p_u0", [128, 512]), psum("p_u1", [128, 512])]

    for kc in range(8):
        p.op("sync", lambda e, kc=kc: e.dma_start(
            out=xqT[:, kc * RPC:(kc + 1) * RPC],
            in_=d_xq[:, kc * 128:(kc + 1) * 128], transpose=True),
            waits=pa, incs=[(s_dma, 16)])
        p.op("sync", lambda e, kc=kc: e.dma_start(
            out=wgT[:, kc * FI:(kc + 1) * FI],
            in_=d_wqgb[:, kc * 128:(kc + 1) * 128], transpose=True),
            waits=pa, incs=[(s_dma, 16)])
        p.op("sync", lambda e, kc=kc: e.dma_start(
            out=wuT[:, kc * FI:(kc + 1) * FI],
            in_=d_wqub[:, kc * 128:(kc + 1) * 128], transpose=True),
            waits=pa, incs=[(s_dma, 16)])
    pb_ld = s_dma.n
    p.op("scalar", lambda e: e.activation(
        out=dly[:, :], in_=dly[:, :], func=AF.Square),
        waits=[(s_dma, pb_ld)], incs=[(s_act, 1)])
    pb_gate = s_act.n

    NCH = [(i * 512, 512) for i in range(5)] + [(2560, 256)]
    silu_b = {}
    httm = {}
    httm_t = {}
    hq_dma = {}
    habs = {}
    qfin = {}
    cig_box = [0]

    def b_chunks(tt):
        par = tt % 2
        for ci, (n0, nw) in enumerate(NCH):
            cig = cig_box[0]
            bank = cig % 2

            def gfn(e, tt=tt, n0=n0, nw=nw, bank=bank):
                ins = None
                for kc in range(8):
                    ins = e.matmul(
                        p_g[bank][:, :nw],
                        xqT[:, kc * RPC + tt * 128:kc * RPC + (tt + 1) * 128],
                        wgT[:, kc * FI + n0:kc * FI + n0 + nw],
                        start=(kc == 0), stop=(kc == 7))
                return ins

            def ufn(e, tt=tt, n0=n0, nw=nw, bank=bank):
                ins = None
                for kc in range(8):
                    ins = e.matmul(
                        p_u[bank][:, :nw],
                        xqT[:, kc * RPC + tt * 128:kc * RPC + (tt + 1) * 128],
                        wuT[:, kc * FI + n0:kc * FI + n0 + nw],
                        start=(kc == 0), stop=(kc == 7))
                return ins
            wg_ = [(s_dma, pb_ld), (s_act, pb_gate)]
            if cig >= 2:
                wg_.append((s_act, silu_b[cig - 2]))
            p.op("tensor", gfn, waits=wg_, incs=[(s_pe, 1)])
            wu_ = []
            if cig >= 2:
                wu_.append((s_dve, httm[cig - 2]))
            p.op("tensor", ufn, waits=wu_, incs=[(s_pe, 1)])
            pe_now = s_pe.n
            wsil = [(s_pe, pe_now - 1)]
            if cig >= 1:
                wsil.append((s_dve, httm[cig - 1]))
            p.op("scalar", lambda e, nw=nw, bank=bank: e.activation(
                out=sgf[:, :nw], in_=p_g[bank][:, :nw],
                func=AF.Silu),
                waits=wsil, incs=[(s_act, 1)])
            silu_b[cig] = s_act.n
            wht = [(s_act, silu_b[cig]), (s_pe, pe_now), (s_dve, s_dve.n)]
            p.op("vector", lambda e, nw=nw, bank=bank, n0=n0, par=par:
                 e.tensor_tensor(
                     out=htf[:, par * FI + n0:par * FI + n0 + nw],
                     in0=sgf[:, :nw], in1=p_u[bank][:, :nw], op=OP.mult),
                 waits=wht, incs=[(s_dve, 1)])
            httm[cig] = s_dve.n
            cig_box[0] = cig + 1
        httm_t[tt] = s_dve.n

    qxn = {}
    hsgn = {}

    def b_quant_abs(pv):
        pvpar = pv % 2
        w = [(s_dve, max(httm_t[pv], qxn.get(pv - 1, 0)))]
        habs[pv] = emit_quant2_abs(
            128, FI, htf[:, pvpar * FI:(pvpar + 1) * FI], QTB, w)

    def b_quant_rest(pv):
        pvpar = pv % 2
        hsgn[pv] = emit_quant2_sign(
            128, FI, htf[:, pvpar * FI:(pvpar + 1) * FI], QTB,
            [(s_dve, max(httm_t[pv], qfin.get(pv - 1, 0)))])
        wr = []
        if pv >= 1:
            wr.append((s_dmas, hq_dma[pv - 1]))
        box = [0]
        qfin[pv] = emit_quant2_rest(128, FI, htq[:, :], QTB, habs[pv],
                                    hsgn[pv], wr, box)
        qxn[pv] = box[0]
        p.op("gpsimd", lambda e, pv=pv: e.dma_start(
            out=d_hq[pv * 128:(pv + 1) * 128, :], in_=htq[:, :]),
            waits=[(s_dve, qfin[pv])], incs=[(s_dmas, 16)])
        hq_dma[pv] = s_dmas.n

    for tt in range(32):
        if tt >= 1:
            b_quant_abs(tt - 1)
        b_chunks(tt)
        if tt >= 1:
            b_quant_rest(tt - 1)
    b_quant_abs(31)
    b_quant_rest(31)

    if dbg_tap("hqtap", [RPC, FI], BF16) is not None:
        p.op("gpsimd", lambda e: e.dma_start(out=dbg["hqtap"], in_=d_hq),
             waits=[(s_dmas, s_dmas.n)], incs=[(s_out, 16)])
    pb = [(s_dmas, s_dmas.n), (s_dma, s_dma.n), (s_dve, s_dve.n),
          (s_act, s_act.n), (s_pe, s_pe.n)]
    phase_end()

    hqT = sbuf("hqT", [128, 2 * 22 * 512], BF16)
    wdT = sbuf("wdT", [128, 22 * DM], BF16)
    c_a1 = sbuf("c_a1", [128, 2 * DM], F32)
    c_a2 = sbuf("c_a2", [128, 2 * DM], F32)
    c_x1 = sbuf("c_x1", [128, 2 * DM], BF16)
    c_ln = sbuf("c_ln", [128, 16], F32)
    c_o = sbuf("c_o", [128, 2 * DM], F32)
    epsC = sbuf("epsC", [128, 1], F32)
    p_d = [[psum("p_d00", [128, 512]), psum("p_d01", [128, 512])],
           [psum("p_d10", [128, 512]), psum("p_d11", [128, 512])]]

    for kc in range(22):
        p.op("sync", lambda e, kc=kc: e.dma_start(
            out=wdT[:, kc * DM:(kc + 1) * DM],
            in_=d_wqdb[:, kc * 128:(kc + 1) * 128], transpose=True),
            waits=pb, incs=[(s_dma, 16)])
    wd_ld = s_dma.n
    p.op("vector", lambda e: e.memset(epsC[:, :], EPS), waits=pb,
         incs=[(s_dve, 1)])

    pe_sup = {}
    cres = {}
    cxm = {}
    cx_n = {}
    mm_m = {}
    sq_m = {}
    rstd_m = {}
    outm = {}
    hq_lds = {}
    hq_gates = {}

    def c_loads(sup):
        spar = sup % 2
        st0 = sup * 512
        w = [(s_dma, wd_ld)]
        if sup >= 2:
            w.append((s_pe, pe_sup[sup - 2]))
        for kc in range(22):
            p.op("sync", lambda e, kc=kc, st0=st0, spar=spar: e.dma_start(
                out=hqT[:, spar * 22 * 512 + kc * 512:
                        spar * 22 * 512 + (kc + 1) * 512],
                in_=d_hq[st0:st0 + 512, kc * 128:(kc + 1) * 128],
                transpose=True),
                waits=(w if kc == 0 else []),
                incs=[(s_hq, 16)])
        hq_lds[sup] = s_hq.n
        p.op("scalar", lambda e: e.activation(
            out=dly[:, :], in_=dly[:, :], func=AF.Square),
            waits=[(s_hq, hq_lds[sup])], incs=[(s_act, 1)])
        hq_gates[sup] = s_act.n

    def c_mm(tt):
        sup, ttl, par = tt // 4, tt % 4, tt % 2
        spar = sup % 2
        psl = slice(par * DM, (par + 1) * DM)
        for fh in range(2):
            def dfn(e, ttl=ttl, fh=fh, par=par, spar=spar):
                ins = None
                for kc in range(22):
                    ins = e.matmul(
                        p_d[par][fh][:, :],
                        hqT[:, spar * 22 * 512 + kc * 512 + ttl * 128:
                            spar * 22 * 512 + kc * 512 + (ttl + 1) * 128],
                        wdT[:, kc * DM + fh * 512:kc * DM + (fh + 1) * 512],
                        start=(kc == 0), stop=(kc == 21))
                return ins
            w2 = [(s_hq, hq_lds[sup]), (s_act, hq_gates[sup])]
            if tt >= 2:
                w2.append((s_dve, cres[tt - 2]))
            p.op("tensor", dfn, waits=w2, incs=[(s_pe, 1)])
        mm_m[tt] = s_pe.n
        pe_sup[sup] = s_pe.n
        p.op("scalar", lambda e, tt=tt, psl=psl: e.dma_start(
            out=c_x1[:, psl], in_=d_x1[tt * 128:(tt + 1) * 128, :]),
            waits=([(s_dve, cxm[tt - 2])] if tt >= 2 else []),
            incs=[(s_cx, 16)])
        cx_n[tt] = s_cx.n

    def c_s1(tt):
        par = tt % 2
        psl = slice(par * DM, (par + 1) * DM)
        p.op("vector", lambda e, psl=psl: e.tensor_copy(
            out=c_a2[:, psl], in_=c_x1[:, psl]),
            waits=[(s_cx, cx_n[tt])], incs=[(s_dve, 1)])
        cxm[tt] = s_dve.n
        p.op("vector", lambda e, par=par: e.tensor_tensor(
            out=c_a1[:, par * DM:par * DM + 512],
            in0=c_a2[:, par * DM:par * DM + 512], in1=p_d[par][0][:, :],
            op=OP.add), waits=[(s_pe, mm_m[tt]), (s_dve, s_dve.n)],
            incs=[(s_dve, 1)])
        p.op("vector", lambda e, par=par: e.tensor_tensor(
            out=c_a1[:, par * DM + 512:par * DM + 1024],
            in0=c_a2[:, par * DM + 512:par * DM + 1024],
            in1=p_d[par][1][:, :], op=OP.add),
            waits=[(s_dve, s_dve.n)], incs=[(s_dve, 1)])
        cres[tt] = s_dve.n
        p.op("vector", lambda e, par=par, psl=psl: e.reduce_sum(
            out=c_ln[:, par * 8:par * 8 + 1], in_=c_a1[:, psl],
            axis=mybir.AxisListType.X),
            waits=[(s_dve, s_dve.n)], incs=[(s_dve, 1)])
        p.op("vector", lambda e, par=par: e.tensor_single_scalar(
            out=c_ln[:, par * 8:par * 8 + 1],
            in_=c_ln[:, par * 8:par * 8 + 1], scalar=1.0 / DM,
            op=OP.mult), waits=[(s_dve, s_dve.n)], incs=[(s_dve, 1)])
        p.op("scalar", lambda e, psl=psl: e.activation(
            out=c_a2[:, psl], in_=c_a1[:, psl], func=AF.Square),
            waits=[(s_dve, s_dve.n)], incs=[(s_act, 1)])
        sq_m[tt] = s_act.n

    def c_s2(tt):
        par = tt % 2
        psl = slice(par * DM, (par + 1) * DM)
        p.op("vector", lambda e, par=par, psl=psl: e.reduce_sum(
            out=c_ln[:, par * 8 + 1:par * 8 + 2], in_=c_a2[:, psl],
            axis=mybir.AxisListType.X),
            waits=[(s_act, sq_m[tt])], incs=[(s_dve, 1)])
        p.op("vector", lambda e, par=par: e.tensor_scalar(
            out=c_ln[:, par * 8 + 2:par * 8 + 3],
            in0=c_ln[:, par * 8:par * 8 + 1],
            scalar1=c_ln[:, par * 8:par * 8 + 1], scalar2=None,
            op0=OP.mult),
            waits=[(s_dve, s_dve.n)], incs=[(s_dve, 1)])
        p.op("vector", lambda e, par=par: e.tensor_scalar(
            out=c_ln[:, par * 8 + 3:par * 8 + 4],
            in0=c_ln[:, par * 8 + 1:par * 8 + 2], scalar1=1.0 / DM,
            scalar2=c_ln[:, par * 8 + 2:par * 8 + 3], op0=OP.mult,
            op1=OP.subtract),
            waits=[(s_dve, s_dve.n)], incs=[(s_dve, 1)])
        p.op("scalar", lambda e, par=par: e.activation(
            out=c_ln[:, par * 8 + 4:par * 8 + 5],
            in_=c_ln[:, par * 8 + 3:par * 8 + 4], func=AF.Ln,
            bias=epsC[:, 0:1]),
            waits=[(s_dve, s_dve.n)], incs=[(s_act, 1)])
        p.op("scalar", lambda e, par=par: e.activation(
            out=c_ln[:, par * 8 + 5:par * 8 + 6],
            in_=c_ln[:, par * 8 + 4:par * 8 + 5], func=AF.Exp,
            scale=-0.5),
            waits=[(s_act, s_act.n)], incs=[(s_act, 1)])
        rstd_m[tt] = s_act.n

    def c_s3(tt):
        par = tt % 2
        psl = slice(par * DM, (par + 1) * DM)
        wo2 = [(s_act, rstd_m[tt]), (s_dve, s_dve.n)]
        if tt >= 2:
            wo2.append((s_out, outm[tt - 2]))
        p.op("vector", lambda e, par=par, psl=psl: e.tensor_scalar(
            out=c_o[:, psl], in0=c_a1[:, psl],
            scalar1=c_ln[:, par * 8:par * 8 + 1],
            scalar2=c_ln[:, par * 8 + 5:par * 8 + 6],
            op0=OP.subtract, op1=OP.mult),
            waits=wo2, incs=[(s_dve, 1)])
        p.op("gpsimd", lambda e, tt=tt, psl=psl: e.dma_start(
            out=out[tt * 128:(tt + 1) * 128, :], in_=c_o[:, psl]),
            waits=[(s_dve, s_dve.n)], incs=[(s_out, 16)])
        outm[tt] = s_out.n

    for tt in range(32):
        if tt % 4 == 0:
            c_loads(tt // 4)
        c_mm(tt)
        c_s1(tt)
        if tt >= 1:
            c_s3(tt - 1)
        c_s2(tt)
    c_s3(31)

    final = s_out.n
    p.op("gpsimd", lambda e: e.engine_nop(), waits=[(s_out, final)])

    phase_end()
    with nc.Block() as block:
        p.emit(block)
    stack.close()
    return nc, dbg


def _host_prep(inputs):
    x = np.ascontiguousarray(np.asarray(inputs["L0_attn_x"], np.float32))
    xr = x.reshape(B * L, DM)
    w_in = np.ascontiguousarray(np.asarray(inputs["L0_attn_in_proj_w"], np.float32))
    conv_w = np.ascontiguousarray(
        np.asarray(inputs["L0_attn_conv_w"], np.float32)
        .reshape(16, 128, DC).transpose(1, 0, 2).reshape(128, 16 * DC))
    conv_b = np.ascontiguousarray(
        np.asarray(inputs["L0_attn_conv_b"], np.float32)
        .reshape(16, 128).transpose(1, 0))
    w_xp = np.ascontiguousarray(np.asarray(inputs["L0_attn_x_proj_w"], np.float32))
    dt_w_full = np.asarray(inputs["L0_attn_dt_proj_w"], np.float32)[:, 0]
    dt_b_full = np.asarray(inputs["L0_attn_dt_proj_b"], np.float32)
    a_log_full = np.asarray(inputs["L0_attn_A_log"], np.float32)
    d_par_full = np.asarray(inputs["L0_attn_D_param"], np.float32)
    w_out = np.ascontiguousarray(np.asarray(inputs["L0_attn_out_proj_w"], np.float32))
    w_g = np.ascontiguousarray(np.asarray(inputs["L0_ffn_w_gate"], np.float32))
    w_u = np.ascontiguousarray(np.asarray(inputs["L0_ffn_w_up"], np.float32))
    w_d = np.ascontiguousarray(np.asarray(inputs["L0_ffn_w_down"], np.float32))

    in_maps = []
    for j in range(NCORES):
        r0 = j * RPC
        if j % 2 == 0:
            halo = np.zeros((3, DM), np.float32)
        else:
            halo = xr[r0 - 3:r0]
        tailpad = np.zeros((RH - 3 - RPC, DM), np.float32)
        x_rows = np.ascontiguousarray(
            np.concatenate([halo, xr[r0:r0 + RPC], tailpad], 0))
        ds0 = j * DPC
        in_maps.append({
            "x_rows": x_rows,
            "w_in": w_in,
            "conv_w": conv_w,
            "conv_b": conv_b,
            "w_xp": w_xp,
            "dt_w": np.ascontiguousarray(
                dt_w_full[ds0:ds0 + DPC].reshape(2, 128).transpose(1, 0)),
            "dt_b": np.ascontiguousarray(
                dt_b_full[ds0:ds0 + DPC].reshape(2, 128).transpose(1, 0)),
            "a_log": np.ascontiguousarray(
                a_log_full[ds0:ds0 + DPC].reshape(2, 128, DS)
                .transpose(1, 0, 2).reshape(128, 2 * DS)),
            "d_par": np.ascontiguousarray(
                d_par_full[ds0:ds0 + DPC].reshape(2, 128).transpose(1, 0)),
            "w_out": w_out,
            "w_g": w_g,
            "w_u": w_u,
            "w_d": w_d,
        })
    return in_maps


def kernel(**inputs):
    in_maps = _host_prep(inputs)
    nc, _ = build_program()
    res = run_bass_kernel_spmd(nc, in_maps, core_ids=list(range(NCORES)))
    outs = [np.asarray(res.results[j]["out"], np.float32) for j in range(NCORES)]
    return np.concatenate(outs, 0).reshape(B, L, DM)
